# version 02
# baseline (speedup 1.0000x reference)
import numpy as np
import ml_dtypes
import concourse.bass as bass
import concourse.mybir as mybir
from concourse.bass_utils import run_bass_kernel_spmd
from concourse.alu_op_type import AluOpType as ALU

F32 = mybir.dt.float32
BF16 = mybir.dt.bfloat16
FP16 = mybir.dt.float16
I32 = mybir.dt.int32
AF = mybir.ActivationFunctionType
AX = mybir.AxisListType

D = 2048
S = 4096
NPT1 = 32
UNITS = 4
RUN = [1024, 2048, 3072, 4096]
CHUNKS = [[0, 3, 4, 7], [1, 2, 5, 6]]
EPS = 1e-5
INV2048 = 1.0 / 2048.0


class Sch:
    CH = ("T", "V", "A", "DS", "DG")

    def __init__(self):
        self.ops = {c: [] for c in self.CH}

    def add(self, ch, fn, deps=(), sdeps=()):
        deps = [d for d in deps if d is not None]
        sdeps = [d for d in sdeps if d is not None]
        idx = len(self.ops[ch])
        self.ops[ch].append({"fn": fn, "deps": deps, "sdeps": sdeps})
        return (ch, idx)

    def last(self, ch):
        return (ch, len(self.ops[ch]) - 1) if self.ops[ch] else None

    def finalize(self):
        self.marked = {c: set() for c in self.CH}
        for c in ("DS", "DG"):
            self.marked[c] = set(range(len(self.ops[c])))
        for c in self.CH:
            for op in self.ops[c]:
                for (dc, di) in op["deps"] + op["sdeps"]:
                    if dc != c and dc not in ("DS", "DG"):
                        self.marked[dc].add(di)
        self.incord = {}
        for c in self.CH:
            cnt, m = 0, []
            for i in range(len(self.ops[c])):
                if i in self.marked[c]:
                    cnt += 1
                m.append(cnt)
            self.incord[c] = m

    def emit(self, ch, eng, sems):
        watermark = {c: 0 for c in self.CH}
        last_barrier = -1
        isdma = ch in ("DS", "DG")
        for i, op in enumerate(self.ops[ch]):
            need_drain = False
            for (dc, di) in op["sdeps"]:
                if dc == ch and not isdma:
                    if di > last_barrier and di >= i - 12:
                        need_drain = True
                else:
                    mult = 16 if dc in ("DS", "DG") else 1
                    tgt = mult * self.incord[dc][di]
                    if tgt > watermark[dc]:
                        eng.wait_ge(sems[dc], tgt)
                        watermark[dc] = tgt
            for (dc, di) in op["deps"]:
                if dc == ch and not isdma:
                    if di > last_barrier and di >= i - 1:
                        need_drain = True
                else:
                    mult = 16 if dc in ("DS", "DG") else 1
                    tgt = mult * self.incord[dc][di]
                    if tgt > watermark[dc]:
                        eng.wait_ge(sems[dc], tgt)
                        watermark[dc] = tgt
            if need_drain:
                eng.drain()
                last_barrier = i - 1
            ins = op["fn"](eng)
            if i in self.marked[ch] and ins is not None:
                ins.then_inc(sems[ch], 16 if isdma else 1)


def build_nc():
    import contextlib
    nc = bass.Bass("TRN2", target_bir_lowering=False)
    sch = Sch()
    stack = contextlib.ExitStack()
    stack1 = contextlib.ExitStack()

    xin = nc.dram_tensor("xin", [S, D], F32, kind="ExternalInput")
    xq_own = nc.dram_tensor("xq_own", [2048, D], F32, kind="ExternalInput")
    wqT = nc.dram_tensor("wqT", [D, D], F32, kind="ExternalInput")
    wkvT = nc.dram_tensor("wkvT", [D, 1024], F32, kind="ExternalInput")
    woT = nc.dram_tensor("woT", [D, D], F32, kind="ExternalInput")
    normw_h = nc.dram_tensor("normw", [128, D], F32, kind="ExternalInput")
    eye_h = nc.dram_tensor("eye", [128, 128], BF16, kind="ExternalInput")
    eye32_h = nc.dram_tensor("eye32", [128, 128], F32, kind="ExternalInput")
    posk_h = nc.dram_tensor("posk", [128, NPT1 * 64], F32, kind="ExternalInput")
    posq_h = nc.dram_tensor("posq", [128, 16 * 64], F32, kind="ExternalInput")
    jrow_h = nc.dram_tensor("jrow", [128, 64], F32, kind="ExternalInput")
    masks_h = nc.dram_tensor("masks", [UNITS * 8 * 128, 512], BF16, kind="ExternalInput")
    wselT_h = nc.dram_tensor("wselT", [D, 64], F32, kind="ExternalInput")
    updT_h = nc.dram_tensor("updT", [D, 64], F32, kind="ExternalInput")
    w2aug_h = nc.dram_tensor("w2aug", [72, D], F32, kind="ExternalInput")
    mudT_h = nc.dram_tensor("mudT", [D, 8], F32, kind="ExternalInput")
    debT_h = nc.dram_tensor("debT", [D, 8], F32, kind="ExternalInput")
    slopeb_h = nc.dram_tensor("slopeb", [128, 8], F32, kind="ExternalInput")
    centb_h = nc.dram_tensor("centb", [128, 8], F32, kind="ExternalInput")
    nvalsb_h = nc.dram_tensor("nvalsb", [128, 8], F32, kind="ExternalInput")
    nmBb_h = nc.dram_tensor("nmBb", [128, 8], F32, kind="ExternalInput")
    diagns_h = nc.dram_tensor("diagns", [64, 8], BF16, kind="ExternalInput")
    diaghn_h = nc.dram_tensor("diaghn", [64, 8], BF16, kind="ExternalInput")
    out_h = nc.dram_tensor("out", [2048, D], F32, kind="ExternalOutput")
    wq_bf = nc.dram_tensor("wq_bf", [D, D], BF16)
    wo_bf = nc.dram_tensor("wo_bf", [D, D], BF16)

    def sb(name, shape, dt):
        return stack.enter_context(nc.sbuf_tensor(name, shape, dt))

    def sb1(name, shape, dt):
        return stack1.enter_context(nc.sbuf_tensor(name, shape, dt))

    def psum(name, shape, dt):
        return stack.enter_context(nc.psum_tensor(name, shape, dt))

    K_sb = sb("K_sb", [128, 4 * S], BF16)
    V_sb = sb("V_sb", [128, NPT1 * 512], BF16)
    cosq = sb("cosq", [128, 16 * 64], BF16)
    sinq = sb("sinq", [128, 16 * 64], BF16)
    eye_sb = sb("eye_sb", [128, 128], BF16)
    eye32_sb = sb("eye32_sb", [128, 128], F32)
    ones_sb = sb("ones_sb", [128, 1], BF16)
    ones_row = sb("ones_row", [1, 128], BF16)
    normw_sb = sb("normw_sb", [128, D], BF16)
    ffnW = sb("ffnW", [128, 16 * 136], BF16)
    debT_sb = sb("debT_sb", [128, 16 * 8], BF16)
    w2aug_sb = sb("w2aug_sb", [128, D], BF16)
    cselb = sb("cselb", [128, 64], F32)
    cupdb = sb("cupdb", [128, 64], F32)
    musqb = sb("musqb", [128, 8], F32)
    slopeb_sb = sb("slopeb_sb", [128, 8], F32)
    centb_sb = sb("centb_sb", [128, 8], F32)
    nvalsb_sb = sb("nvalsb_sb", [128, 8], F32)
    nmBb_sb = sb("nmBb_sb", [128, 8], F32)
    diagns_sb = sb("diagns_sb", [64, 8], BF16)
    diaghn_sb = sb("diaghn_sb", [64, 8], BF16)
    consts = sb("consts", [128, 8], F32)
    scr_a = sb("scr_a", [128, D], F32)
    scr_i = sb("scr_i", [128, D // 2], I32)
    scr_c = sb("scr_c", [128, D], BF16)
    ropeT = sb("ropeT", [128, 1024], BF16)
    stats = sb("stats", [128, 32], F32)
    svec = sb("svec", [128, 16], F32)
    gate_s = sb("gate_s", [128, 128], F32)
    rd_all = sb("rd_all", [128, 64], F32)
    zaugT_sb = sb("zaugT_sb", [128, 128], BF16)
    outT_sb = sb("outT_sb", [128, 512], BF16)

    S0 = psum("S0", [128, 512], F32)
    S1 = psum("S1", [128, 512], F32)
    S2 = psum("S2", [128, 512], F32)
    P0 = psum("P0", [128, 512], F32)
    P1 = psum("P1", [128, 512], F32)
    T0 = psum("T0", [128, 1024], BF16)
    T1 = psum("T1", [128, 1024], BF16)
    M0 = psum("M0", [128, 512], F32)
    SBANK = [S0, S1, S2]
    PBANK = [P0, P1]
    TBANK = [T0, T1]

    bank_free = {}

    def bfree(b):
        return list(bank_free.get(id(b), []))

    def bset(b, evs):
        bank_free[id(b)] = [e for e in evs if e is not None]

    tstate = {"i": 0}

    def talloc():
        b = TBANK[tstate["i"] % 2]
        tstate["i"] += 1
        return b

    e_eye = sch.add("DS", lambda e: e.dma_start(out=eye_sb[:], in_=eye_h.ap()))
    e_eye32 = sch.add("DS", lambda e: e.dma_start(out=eye32_sb[:], in_=eye32_h.ap()))
    e_slope = sch.add("DS", lambda e: e.dma_start(out=slopeb_sb[:], in_=slopeb_h.ap()))
    e_cent = sch.add("DS", lambda e: e.dma_start(out=centb_sb[:], in_=centb_h.ap()))
    e_nvals = sch.add("DS", lambda e: e.dma_start(out=nvalsb_sb[:], in_=nvalsb_h.ap()))
    e_nmB = sch.add("DS", lambda e: e.dma_start(out=nmBb_sb[:], in_=nmBb_h.ap()))
    e_dns = sch.add("DS", lambda e: e.dma_start(out=diagns_sb[:], in_=diagns_h.ap()))
    e_dhn = sch.add("DS", lambda e: e.dma_start(out=diaghn_sb[:], in_=diaghn_h.ap()))
    e_normw = sch.add("DG", lambda e: e.dma_start(out=normw_sb[:], in_=normw_h.ap()))

    def _hs(t, pstep, kstep, inner):
        return bass.AP(t, 0, [[pstep, 128], [kstep, 16], [1, inner]])

    e_fw = []
    e_fw.append(sch.add("DG", lambda e: e.dma_start(
        out=bass.AP(ffnW, 0, [[16 * 136, 128], [136, 16], [1, 64]]), in_=_hs(wselT_h, 64, 64 * 128, 64))))
    e_fw.append(sch.add("DG", lambda e: e.dma_start(
        out=bass.AP(ffnW, 64, [[16 * 136, 128], [136, 16], [1, 64]]), in_=_hs(updT_h, 64, 64 * 128, 64))))
    e_fw.append(sch.add("DG", lambda e: e.dma_start(
        out=bass.AP(ffnW, 128, [[16 * 136, 128], [136, 16], [1, 8]]), in_=_hs(mudT_h, 8, 8 * 128, 8))))
    e_fw.append(sch.add("DG", lambda e: e.dma_start(
        out=bass.AP(debT_sb, 0, [[16 * 8, 128], [8, 16], [1, 8]]), in_=_hs(debT_h, 8, 8 * 128, 8))))
    e_fw.append(sch.add("DG", lambda e: e.dma_start(out=w2aug_sb[0:72, :], in_=w2aug_h.ap())))
    e_wqbf = sch.add("DG", lambda e: e.dma_start(out=wq_bf.ap(), in_=wqT.ap()))
    e_wobf = sch.add("DG", lambda e: e.dma_start(out=wo_bf.ap(), in_=woT.ap()))

    def _mkc(e):
        e.memset(consts[:, 0:1], 128.0)
        e.memset(consts[:, 1:2], 127.0)
        e.memset(consts[:, 2:3], 1.0)
        e.memset(consts[:, 4:5], EPS)
        e.memset(consts[:, 6:7], 500000.0)
        return e.memset(consts[:, 7:8], 0.0)
    ec0 = sch.add("V", _mkc)
    ec1 = sch.add("A", lambda e: e.activation(out=consts[:, 0:1], in_=consts[:, 7:8], func=AF.Sqrt,
                                              scale=0.0, bias=consts[:, 0:1]), deps=[ec0])
    ec2 = sch.add("V", lambda e: e.reciprocal(out=consts[:, 0:1], in_=consts[:, 0:1]), deps=[ec1])
    ec3 = sch.add("V", lambda e: e.reciprocal(out=consts[:, 1:2], in_=consts[:, 1:2]), deps=[ec0, ec2])
    ec4 = sch.add("A", lambda e: e.activation(out=consts[:, 5:6], in_=consts[:, 7:8], func=AF.Arctan,
                                              scale=0.0, bias=consts[:, 2:3]), deps=[ec0, ec3])
    ec5 = sch.add("V", lambda e: e.tensor_scalar(out=consts[:, 2:3], in0=consts[:, 5:6], scalar1=8.0,
                                                 scalar2=None, op0=ALU.mult), deps=[ec4])
    ec6 = sch.add("V", lambda e: e.tensor_scalar(out=consts[:, 5:6], in0=consts[:, 5:6], scalar1=2.0,
                                                 scalar2=None, op0=ALU.mult), deps=[ec4, ec5])
    ec7 = sch.add("V", lambda e: e.reciprocal(out=consts[:, 3:4], in_=consts[:, 2:3]), deps=[ec5, ec6])
    ec8 = sch.add("A", lambda e: e.activation(out=consts[:, 6:7], in_=consts[:, 7:8], func=AF.Ln,
                                              scale=0.0, bias=consts[:, 6:7]), deps=[ec7, ec0])
    e_ones = sch.add("V", lambda e: e.memset(ones_sb[:], 1.0), deps=[ec0])
    e_ones2 = sch.add("V", lambda e: e.memset(ones_row[:], 1.0), deps=[ec0])

    cosk = sb1("cosk", [128, NPT1 * 64], BF16)
    sink = sb1("sink", [128, NPT1 * 64], BF16)
    tabw = sb1("tabw", [128, NPT1 * 64], F32)
    tabw2 = sb1("tabw2", [128, NPT1 * 64], F32)
    tabw3 = sb1("tabw3", [128, NPT1 * 64], I32)
    freqb = sb1("freqb", [128, 64], F32)
    e_posk = sch.add("DS", lambda e: e.dma_start(out=tabw[:], in_=posk_h.ap()))
    e_jrow = sch.add("DS", lambda e: e.dma_start(out=freqb[:], in_=jrow_h.ap()))
    ef0 = sch.add("V", lambda e: e.tensor_scalar(out=freqb[:], in0=freqb[:], scalar1=consts[:, 6:7],
                                                 scalar2=-0.015625, op0=ALU.mult, op1=ALU.mult),
                  deps=[e_jrow, ec8])
    ef1 = sch.add("A", lambda e: e.activation(out=freqb[:], in_=freqb[:], func=AF.Exp), deps=[ef0])

    def build_tab(width, cos_dst, sin_dst, dep_pos):
        e0 = sch.add("V", lambda e: e.tensor_tensor(
            out=tabw2[:, 0:width], in0=tabw[:, 0:width],
            in1=bass.AP(freqb, 0, [[64, 128], [0, width // 64], [1, 64]]), op=ALU.mult),
            deps=[dep_pos, ef1])
        out = []
        for which, dst in (("sin", sin_dst), ("cos", cos_dst)):
            if which == "cos":
                base = sch.add("V", lambda e: e.tensor_scalar(out=tabw2[:, 0:width], in0=tabw2[:, 0:width],
                                                              scalar1=consts[:, 5:6], scalar2=None,
                                                              op0=ALU.add), deps=[e0] + out, sdeps=[ec6])
            else:
                base = e0
            e1 = sch.add("V", lambda e: e.tensor_scalar(out=tabw[:, 0:width], in0=tabw2[:, 0:width],
                                                        scalar1=consts[:, 3:4], scalar2=None,
                                                        op0=ALU.mult), deps=[base], sdeps=[ec7])
            e2 = sch.add("V", lambda e: e.tensor_copy(out=tabw3[:, 0:width], in_=tabw[:, 0:width]), deps=[e1])
            e3 = sch.add("V", lambda e: e.tensor_scalar(out=tabw[:, 0:width], in0=tabw3[:, 0:width],
                                                        scalar1=consts[:, 2:3], scalar2=None,
                                                        op0=ALU.mult), deps=[e2], sdeps=[ec5])
            e4 = sch.add("V", lambda e: e.tensor_tensor(out=tabw[:, 0:width], in0=tabw2[:, 0:width],
                                                        in1=tabw[:, 0:width], op=ALU.subtract), deps=[e3])
            e5 = sch.add("A", lambda e, dst=dst: e.activation(out=dst[:, 0:width], in_=tabw[:, 0:width],
                                                              func=AF.Sin), deps=[e4])
            out.append(e5)
        return out
    ev_tabk = build_tab(NPT1 * 64, cosk, sink, e_posk)
    e_posq = sch.add("DS", lambda e: e.dma_start(out=tabw[:, 0:1024], in_=posq_h.ap()), deps=ev_tabk)
    ev_tabq = build_tab(1024, cosq, sinq, e_posq)

    def diag_const(lhs_off, lhs_w, rhs_ap, diag_sb, dstb, deps):
        evs = []
        for c in range(16):
            def mk(c=c):
                def f(e):
                    return e.matmul(M0[0:lhs_w, 0:8],
                                    ffnW[:, c * 136 + lhs_off:c * 136 + lhs_off + lhs_w],
                                    rhs_ap(c),
                                    start=(c == 0), stop=(c == 15))
                return f
            evs.append(sch.add("T", mk(), deps=(deps + bfree(M0)) if c == 0 else [evs[-1]]))
        em = sch.add("V", lambda e: e.tensor_tensor(out=gate_s[0:lhs_w, 0:8], in0=M0[0:lhs_w, 0:8],
                                                    in1=diag_sb[0:lhs_w, :], op=ALU.mult),
                     deps=[evs[-1], e_dns, e_dhn])
        er = sch.add("V", lambda e: e.tensor_reduce(out=gate_s[0:lhs_w, 8:9], in_=gate_s[0:lhs_w, 0:8],
                                                    axis=AX.X, op=ALU.add), deps=[em])
        ecst = sch.add("V", lambda e: e.tensor_copy(out=scr_c[0:lhs_w, 0:1], in_=gate_s[0:lhs_w, 8:9]),
                       deps=[er])
        bank = talloc()
        etr = sch.add("T", lambda e, bank=bank: e.transpose(bank[0:1, 0:lhs_w], scr_c[0:lhs_w, 0:1],
                                                            eye_sb[0:lhs_w, 0:lhs_w]),
                      deps=[ecst, e_eye, em] + bfree(bank))
        ecp = sch.add("V", lambda e, bank=bank: e.tensor_copy(out=scr_c[0:1, 256:256 + lhs_w],
                                                              in_=bank[0:1, 0:lhs_w]), deps=[etr])
        bset(bank, [ecp])
        ebc = sch.add("T", lambda e: e.matmul(M0[0:128, 0:lhs_w], ones_row[0:1, 0:128],
                                              scr_c[0:1, 256:256 + lhs_w], start=True, stop=True),
                      deps=[ecp, e_ones2, em])
        eev = sch.add("V", lambda e: e.tensor_copy(out=dstb[:, 0:lhs_w], in_=M0[0:128, 0:lhs_w]), deps=[ebc])
        bset(M0, [eev])
        return eev
    depw = e_fw + [ev_tabk[1], ev_tabq[1]]
    e_csel = diag_const(0, 64, lambda c: ffnW[:, c * 136 + 128:c * 136 + 136], diagns_sb, cselb, depw)
    e_cupd = diag_const(64, 64, lambda c: debT_sb[:, c * 8:(c + 1) * 8], diaghn_sb, cupdb, depw + [e_csel])
    e_musq = diag_const(128, 8, lambda c: ffnW[:, c * 136 + 128:c * 136 + 136], diaghn_sb, musqb,
                        depw + [e_cupd])
    ev_setup = [e_musq]

    def ln_quant(dep_ld, x_ap, xnbf_dst, extra):
        es = []
        for c in range(4):
            es.append(sch.add("V", lambda e, c=c: e.bn_stats(out=stats[:, c * 6:(c + 1) * 6],
                                                             in_=x_ap[:, c * 512:(c + 1) * 512]),
                              deps=[dep_ld] + list(extra)))
        eag = sch.add("V", lambda e: e.bn_aggr(out=svec[:, 0:2], in_=stats[:, 0:24]), deps=es)
        ev1 = sch.add("V", lambda e: e.tensor_scalar(out=svec[:, 2:3], in0=svec[:, 1:2],
                                                     scalar1=consts[:, 4:5], scalar2=None, op0=ALU.add),
                      deps=[eag], sdeps=[ec0])
        ev2 = sch.add("A", lambda e: e.activation(out=svec[:, 2:3], in_=svec[:, 2:3], func=AF.Sqrt),
                      deps=[ev1])
        ev3 = sch.add("V", lambda e: e.reciprocal(out=svec[:, 2:3], in_=svec[:, 2:3]), deps=[ev2])
        exn = sch.add("V", lambda e: e.tensor_scalar(out=scr_a[:], in0=x_ap[:], scalar1=svec[:, 0:1],
                                                     scalar2=svec[:, 2:3], op0=ALU.subtract, op1=ALU.mult),
                      deps=[eag], sdeps=[ev3, eag])
        exnb = None
        if xnbf_dst is not None:
            exnb = sch.add("V", lambda e: e.tensor_copy(out=xnbf_dst, in_=scr_a[:]), deps=[exn])
        em0 = sch.add("V", lambda e: e.reduce_max(out=svec[:, 3:4], in_=scr_a[:], axis=AX.X,
                                                  apply_absolute_value=True), deps=[exn])
        em1 = sch.add("V", lambda e: e.tensor_scalar(out=svec[:, 3:4], in0=svec[:, 3:4], scalar1=EPS,
                                                     scalar2=None, op0=ALU.max), deps=[em0])
        em2 = sch.add("V", lambda e: e.reciprocal(out=svec[:, 4:5], in_=svec[:, 3:4]), deps=[em1])
        em3 = sch.add("V", lambda e: e.tensor_scalar(out=svec[:, 4:5], in0=svec[:, 4:5], scalar1=127.0,
                                                     scalar2=None, op0=ALU.mult), deps=[em2])
        ey = sch.add("V", lambda e: e.tensor_scalar(out=scr_a[:], in0=scr_a[:], scalar1=svec[:, 4:5],
                                                    scalar2=None, op0=ALU.mult),
                     deps=[em0] + ([exnb] if exnb else []), sdeps=[em3])
        eq = None
        for hh in range(2):
            ei = sch.add("V", lambda e, hh=hh: e.tensor_copy(out=scr_i[:],
                                                             in_=scr_a[:, hh * 1024:(hh + 1) * 1024]),
                         deps=[ey] + ([eq] if eq else []))
            eq = sch.add("V", lambda e, hh=hh: e.tensor_scalar(
                out=scr_c[:, hh * 1024:(hh + 1) * 1024], in0=scr_i[:], scalar1=svec[:, 3:4],
                scalar2=consts[:, 1:2], op0=ALU.mult, op1=ALU.mult), deps=[ei], sdeps=[em1, ec3])
        return eq, exnb

    def quant_only(dep_src):
        em0 = sch.add("V", lambda e: e.reduce_max(out=svec[:, 3:4], in_=scr_a[:], axis=AX.X,
                                                  apply_absolute_value=True), deps=[dep_src])
        em1 = sch.add("V", lambda e: e.tensor_scalar(out=svec[:, 3:4], in0=svec[:, 3:4], scalar1=EPS,
                                                     scalar2=None, op0=ALU.max), deps=[em0])
        em2 = sch.add("V", lambda e: e.reciprocal(out=svec[:, 4:5], in_=svec[:, 3:4]), deps=[em1])
        em3 = sch.add("V", lambda e: e.tensor_scalar(out=svec[:, 4:5], in0=svec[:, 4:5], scalar1=127.0,
                                                     scalar2=None, op0=ALU.mult), deps=[em2])
        ey = sch.add("V", lambda e: e.tensor_scalar(out=scr_a[:], in0=scr_a[:], scalar1=svec[:, 4:5],
                                                    scalar2=None, op0=ALU.mult), deps=[em0], sdeps=[em3])
        eq = None
        for hh in range(2):
            ei = sch.add("V", lambda e, hh=hh: e.tensor_copy(out=scr_i[:],
                                                             in_=scr_a[:, hh * 1024:(hh + 1) * 1024]),
                         deps=[ey] + ([eq] if eq else []))
            eq = sch.add("V", lambda e, hh=hh: e.tensor_scalar(
                out=scr_c[:, hh * 1024:(hh + 1) * 1024], in0=scr_i[:], scalar1=svec[:, 3:4],
                scalar2=consts[:, 1:2], op0=ALU.mult, op1=ALU.mult), deps=[ei], sdeps=[em1, ec3])
        return eq

    def transpose_tile(src_t, src_col0, src_dep, dst_t, col_of, extra=()):
        out_evs = []
        for half in range(2):
            bank = talloc()
            trs = []
            for kk in range(8):
                k = half * 8 + kk
                def mk(k=k, kk=kk, bank=bank):
                    def f(e):
                        return e.transpose(bank[:, kk * 128:(kk + 1) * 128],
                                           src_t[:, src_col0 + k * 128:src_col0 + (k + 1) * 128], eye_sb[:])
                    return f
                trs.append(sch.add("T", mk(), deps=[src_dep, e_eye] + bfree(bank) + list(extra)))
            stride = col_of(half * 8 + 1) - col_of(half * 8)
            def mkev(half=half, bank=bank, stride=stride):
                def f(e):
                    dst_ap = bass.AP(dst_t, col_of(half * 8),
                                     [[dst_t.shape[1], 128], [stride, 8], [1, 128]])
                    return e.tensor_copy(out=dst_ap, in_=bank[:, 0:1024])
                return f
            ev = sch.add("V", mkev(), deps=trs)
            bset(bank, [ev])
            out_evs.append(ev)
        return out_evs

    wkv_sb = sb1("wkv_sb", [128, 16 * 1024], BF16)
    x1_sb = sb1("x1_sb", [128, 2 * D], F32)
    XqT1 = sb1("XqT1", [128, 16 * 512], BF16)
    ktmp = sb1("ktmp", [128, 512], BF16)

    e_wkv = sch.add("DG", lambda e: e.dma_start(
        out=bass.AP(wkv_sb, 0, [[16 * 1024, 128], [1024, 16], [1, 1024]]),
        in_=bass.AP(wkvT, 0, [[1024, 128], [128 * 1024, 16], [1, 1024]])))
    evK = [None] * NPT1
    evV = [None] * NPT1
    prev_xq_users = []
    prev_x_reader = {0: None, 1: None}
    prev_ktmp_readers = []

    for t1 in range(8):
        xq_ready = []
        for p in range(4):
            pt = t1 * 4 + p
            buf = pt % 2
            x_ap = x1_sb[:, buf * D:(buf + 1) * D]
            eld = sch.add("DS", lambda e, pt=pt, x_ap=x_ap: e.dma_start(
                out=x_ap, in_=xin.ap()[pt * 128:(pt + 1) * 128, :]), deps=[prev_x_reader[buf]])
            extra = list(prev_xq_users) + (ev_setup if pt == 0 else [])
            eq, _ = ln_quant(eld, x_ap, None, extra)
            prev_x_reader[buf] = eq
            xq_ready += transpose_tile(scr_c, 0, eq, XqT1, lambda k, p=p: k * 512 + p * 128,
                                       extra=prev_xq_users)
        prev_xq_users = []
        for p in range(4):
            pt = t1 * 4 + p
            kps = SBANK[pt % 3]
            vps = PBANK[pt % 2]
            kk = []
            for dk in range(16):
                def mk(dk=dk, p=p, kps=kps):
                    def f(e):
                        return e.matmul(kps[:, 0:512],
                                        XqT1[:, dk * 512 + p * 128:dk * 512 + p * 128 + 128],
                                        wkv_sb[:, dk * 1024:dk * 1024 + 512],
                                        start=(dk == 0), stop=(dk == 15))
                    return f
                d0 = (xq_ready + [e_wkv] + bfree(kps)) if dk == 0 else [kk[-1]]
                kk.append(sch.add("T", mk(), deps=d0))
            ekev = sch.add("A", lambda e, kps=kps: e.copy(out=ktmp[:], in_=kps[:, 0:512]),
                           deps=[kk[-1]] + prev_ktmp_readers)
            bset(kps, [ekev])
            lo_ap = bass.AP(ktmp, 0, [[512, 128], [128, 4], [1, 64]])
            hi_ap = bass.AP(ktmp, 64, [[512, 128], [128, 4], [1, 64]])
            def cs(t, pt):
                return bass.AP(t, pt * 64, [[NPT1 * 64, 128], [0, 4], [1, 64]])
            r1 = sch.add("V", lambda e, pt=pt, lo_ap=lo_ap: e.tensor_tensor(
                out=ropeT[:, 0:256], in0=lo_ap, in1=cs(cosk, pt), op=ALU.mult), deps=[ekev] + ev_tabk)
            r2 = sch.add("V", lambda e, pt=pt, hi_ap=hi_ap: e.tensor_tensor(
                out=ropeT[:, 256:512], in0=hi_ap, in1=cs(sink, pt), op=ALU.mult), deps=[ekev])
            r3 = sch.add("V", lambda e, pt=pt, lo_ap=lo_ap: e.tensor_tensor(
                out=ropeT[:, 512:768], in0=lo_ap, in1=cs(sink, pt), op=ALU.mult), deps=[ekev])
            r4 = sch.add("V", lambda e, pt=pt, hi_ap=hi_ap: e.tensor_tensor(
                out=ropeT[:, 768:1024], in0=hi_ap, in1=cs(cosk, pt), op=ALU.mult), deps=[ekev])
            r5 = sch.add("V", lambda e, lo_ap=lo_ap: e.tensor_tensor(
                out=lo_ap, in0=ropeT[:, 0:256], in1=ropeT[:, 256:512], op=ALU.subtract), deps=[r1, r2, r3])
            r6 = sch.add("V", lambda e, hi_ap=hi_ap: e.tensor_tensor(
                out=hi_ap, in0=ropeT[:, 512:768], in1=ropeT[:, 768:1024], op=ALU.add), deps=[r4, r5])
            bank = talloc()
            ktr = []
            for g in range(4):
                def mk(g=g, bank=bank):
                    def f(e):
                        return e.transpose(bank[:, g * 128:(g + 1) * 128],
                                           ktmp[:, g * 128:(g + 1) * 128], eye_sb[:])
                    return f
                ktr.append(sch.add("T", mk(), deps=[r5, r6] + bfree(bank)))
            def mkev(pt=pt, bank=bank):
                def f(e):
                    dst = bass.AP(K_sb, pt * 128, [[4 * S, 128], [S, 4], [1, 128]])
                    return e.tensor_copy(out=dst, in_=bank[:, 0:512])
                return f
            ekv = sch.add("V", mkev(), deps=ktr)
            bset(bank, [ekv])
            evK[pt] = ekv
            prev_ktmp_readers = ktr
            vv = []
            for dk in range(16):
                def mk(dk=dk, p=p, vps=vps):
                    def f(e):
                        return e.matmul(vps[:, 0:512],
                                        XqT1[:, dk * 512 + p * 128:dk * 512 + p * 128 + 128],
                                        wkv_sb[:, dk * 1024 + 512:dk * 1024 + 1024],
                                        start=(dk == 0), stop=(dk == 15))
                    return f
                d0 = (xq_ready + [e_wkv] + bfree(vps)) if dk == 0 else [vv[-1]]
                vv.append(sch.add("T", mk(), deps=d0))
            evev = sch.add("A", lambda e, vps=vps, pt=pt: e.copy(
                out=V_sb[:, pt * 512:(pt + 1) * 512], in_=vps[:, 0:512]), deps=[vv[-1]])
            bset(vps, [evev])
            evV[pt] = evev
            prev_xq_users += [kk[-1], vv[-1]]

    ph1_bar = [sch.last("T"), sch.last("V"), sch.last("A"), e_wkv]
    stack1.close()

    xn_sb = sb("xn_sb", [128, 4 * D], BF16)
    xnT_sb = sb("xnT_sb", [128, 16 * 128], BF16)
    XqT2 = sb("XqT2", [128, 16 * 512], BF16)
    Q_sb = sb("Q_sb", [128, 4 * D], BF16)
    attn_sb = sb("attn_sb", [128, 4 * D], BF16)
    PT_sb = sb("PT_sb", [128, 3 * 512], BF16)
    mask_sb = sb("mask_sb", [128, 8 * 512], BF16)
    wst_sb = sb("wst_sb", [128, 2 * 1024], BF16)
    x2_sb = sb("x2_sb", [128, 2 * D], F32)
    out_sb = x2_sb

    ev_stores = []
    prev_unit = list(ph1_bar)
    prev_strip_reader = {0: None, 1: None}
    strip_ct = {"i": 0}
    slot_users = {0: [], 1: [], 2: []}
    prev_out_store = {}
    prev_outT_readers = []

    for u in range(UNITS):
        TT = RUN[u] // 128
        mstart = TT - 8
        e_mload = sch.add("DS", lambda e, u=u: e.dma_start(
            out=bass.AP(mask_sb, 0, [[8 * 512, 128], [512, 8], [1, 512]]),
            in_=bass.AP(masks_h, u * 8 * 128 * 512, [[512, 128], [128 * 512, 8], [1, 512]])),
            deps=prev_unit)

        ev_XqT = []
        ev_ffn_done = []
        prev_x2_reader = {0: None, 1: None}
        for p in range(4):
            ptg = u * 4 + p
            buf = p % 2
            x_ap = x2_sb[:, buf * D:(buf + 1) * D]
            eld = sch.add("DS", lambda e, ptg=ptg, x_ap=x_ap: e.dma_start(
                out=x_ap, in_=xq_own.ap()[ptg * 128:(ptg + 1) * 128, :]),
                deps=[prev_x2_reader[buf]] + prev_unit)
            eq, exnb = ln_quant(eld, x_ap, xn_sb[:, p * D:(p + 1) * D], prev_unit)
            es1 = sch.add("V", lambda e: e.tensor_scalar(out=svec[:, 5:6], in0=svec[:, 1:2],
                                                         scalar1=svec[:, 2:3], scalar2=svec[:, 2:3],
                                                         op0=ALU.mult, op1=ALU.mult), deps=[eq], sdeps=[eq])
            es2 = sch.add("V", lambda e: e.tensor_scalar(out=svec[:, 5:6], in0=svec[:, 5:6],
                                                         scalar1=2048.0, scalar2=None, op0=ALU.mult),
                          deps=[es1])
            ev_XqT += transpose_tile(scr_c, 0, eq, XqT2, lambda k, p=p: k * 512 + p * 128, extra=prev_unit)
            ev_xnT_p = transpose_tile(xn_sb, p * D, exnb, xnT_sb, lambda k: k * 128,
                                      extra=prev_unit + ev_ffn_done[-1:])
            fsel = []
            for dk in range(16):
                def mk(dk=dk, p=p):
                    def f(e):
                        return e.matmul(M0[:, 0:136],
                                        xnT_sb[:, dk * 128:(dk + 1) * 128],
                                        ffnW[:, dk * 136:(dk + 1) * 136],
                                        start=(dk == 0), stop=(dk == 15))
                    return f
                d0 = (ev_xnT_p + bfree(M0) + e_fw) if dk == 0 else [fsel[-1]]
                fsel.append(sch.add("T", mk(), deps=d0))
            g0 = sch.add("V", lambda e: e.tensor_tensor(out=gate_s[:, 0:64], in0=M0[:, 0:64],
                                                        in1=cselb[:], op=ALU.subtract),
                         deps=[fsel[-1], e_csel])
            g1 = sch.add("V", lambda e: e.tensor_tensor(out=gate_s[:, 0:64], in0=gate_s[:, 0:64],
                                                        in1=gate_s[:, 0:64], op=ALU.mult), deps=[g0])
            g2 = sch.add("V", lambda e: e.tensor_reduce(
                out=gate_s[:, 64:72], in_=bass.AP(gate_s, 0, [[128, 128], [8, 8], [1, 8]]),
                axis=AX.X, op=ALU.add), deps=[g1])
            g3 = sch.add("V", lambda e: e.scalar_tensor_tensor(out=gate_s[:, 72:80], in0=M0[:, 128:136],
                                                               scalar=-2.0, in1=musqb[:], op0=ALU.mult,
                                                               op1=ALU.add), deps=[fsel[-1], e_musq])
            g4 = sch.add("V", lambda e: e.tensor_scalar(out=gate_s[:, 72:80], in0=gate_s[:, 72:80],
                                                        scalar1=svec[:, 5:6], scalar2=None, op0=ALU.add),
                         deps=[g3], sdeps=[es2])
            g5 = sch.add("V", lambda e: e.reciprocal(out=gate_s[:, 72:80], in_=gate_s[:, 72:80]), deps=[g4])
            g6 = sch.add("V", lambda e: e.tensor_tensor(out=gate_s[:, 64:72], in0=gate_s[:, 64:72],
                                                        in1=gate_s[:, 72:80], op=ALU.mult), deps=[g5, g2])
            g7 = sch.add("V", lambda e: e.tensor_tensor(out=gate_s[:, 64:72], in0=gate_s[:, 64:72],
                                                        in1=centb_sb[:], op=ALU.subtract), deps=[g6, e_cent])
            g8 = sch.add("V", lambda e: e.tensor_tensor(out=gate_s[:, 64:72], in0=gate_s[:, 64:72],
                                                        in1=slopeb_sb[:], op=ALU.mult), deps=[g7, e_slope])
            g9 = sch.add("A", lambda e: e.activation(out=gate_s[:, 80:88], in_=gate_s[:, 64:72],
                                                     func=AF.Sigmoid), deps=[g8])
            g10 = sch.add("V", lambda e: e.reduce_max(out=svec[:, 6:7], in_=gate_s[:, 80:88], axis=AX.X),
                          deps=[g9])
            g11 = sch.add("V", lambda e: e.tensor_tensor(
                out=gate_s[:, 88:96], in0=gate_s[:, 80:88],
                in1=bass.AP(svec, 6, [[16, 128], [0, 8]]), op=ALU.is_equal), deps=[g10])
            g12 = sch.add("V", lambda e: e.tensor_tensor(out=gate_s[:, 88:96], in0=gate_s[:, 88:96],
                                                         in1=nmBb_sb[:], op=ALU.mult), deps=[g11, e_nmB])
            g13 = sch.add("V", lambda e: e.tensor_scalar(out=gate_s[:, 88:96], in0=gate_s[:, 88:96],
                                                         scalar1=1000.0, scalar2=None, op0=ALU.add), deps=[g12])
            g14 = sch.add("V", lambda e: e.tensor_reduce(out=svec[:, 7:8], in_=gate_s[:, 88:96],
                                                         axis=AX.X, op=ALU.min), deps=[g13])
            g15 = sch.add("V", lambda e: e.tensor_tensor(
                out=scr_c[:, 64:72], in0=nvalsb_sb[:],
                in1=bass.AP(svec, 7, [[16, 128], [0, 8]]), op=ALU.is_equal), deps=[g14, e_nvals])
            g16 = sch.add("V", lambda e: e.tensor_tensor(out=gate_s[:, 0:64], in0=M0[:, 64:128],
                                                         in1=cupdb[:], op=ALU.subtract),
                          deps=[fsel[-1], e_cupd, g2])
            bset(M0, [g3, g16])
            g17 = sch.add("V", lambda e: e.tensor_tensor(
                out=scr_c[:, 0:64], in0=gate_s[:, 0:64],
                in1=bass.AP(scr_c, 64, [[D, 128], [0, 8], [1, 8]]), op=ALU.mult), deps=[g16, g15])
            bank = talloc()
            g18 = sch.add("T", lambda e, bank=bank: e.transpose(bank[0:72, 0:128], scr_c[:, 0:72],
                                                                eye_sb[:]), deps=[g17] + bfree(bank))
            g19 = sch.add("V", lambda e, bank=bank: e.tensor_copy(out=zaugT_sb[0:72, 0:128],
                                                                  in_=bank[0:72, 0:128]), deps=[g18])
            bset(bank, [g19])
            modps = SBANK[p % 3]
            last_f = None
            for ncix in range(4):
                def mk(ncix=ncix, modps=modps):
                    def f(e):
                        return e.matmul(modps[:, 0:512], zaugT_sb[0:72, 0:128],
                                        w2aug_sb[0:72, ncix * 512:(ncix + 1) * 512],
                                        start=True, stop=True)
                    return f
                dmm = [g19] + e_fw + (bfree(modps) if ncix == 0 else [last_f])
                emm = sch.add("T", mk(), deps=dmm)
                def mkf1(ncix=ncix, p=p, modps=modps):
                    def f(e):
                        return e.tensor_tensor(out=scr_a[:, ncix * 512:(ncix + 1) * 512],
                                               in0=modps[:, 0:512],
                                               in1=xn_sb[:, p * D + ncix * 512:p * D + (ncix + 1) * 512],
                                               op=ALU.subtract)
                    return f
                ef1 = sch.add("V", mkf1(), deps=[emm, exnb, eq])
                def mkf2(ncix=ncix, p=p):
                    def f(e):
                        sl0 = p * D + ncix * 512
                        return e.scalar_tensor_tensor(out=xn_sb[:, sl0:sl0 + 512],
                                                      in0=scr_a[:, ncix * 512:(ncix + 1) * 512],
                                                      scalar=svec[:, 6:7],
                                                      in1=xn_sb[:, sl0:sl0 + 512],
                                                      op0=ALU.mult, op1=ALU.add)
                    return f
                ef2 = sch.add("V", mkf2(), deps=[ef1, ev_xnT_p[-1]], sdeps=[g10])
                def mkf3(ncix=ncix, p=p, buf=buf):
                    def f(e):
                        sl0 = p * D + ncix * 512
                        return e.tensor_tensor(out=xn_sb[:, sl0:sl0 + 512], in0=xn_sb[:, sl0:sl0 + 512],
                                               in1=x2_sb[:, buf * D + ncix * 512:buf * D + (ncix + 1) * 512],
                                               op=ALU.add)
                    return f
                ef3 = sch.add("V", mkf3(), deps=[ef2])
                last_f = ef3
            bset(modps, [last_f])
            ev_ffn_done.append(last_f)
            prev_x2_reader[buf] = last_f

        ev_rope_q = {}
        qps_l = [S0, S1, S2, P0]
        mm_last = {}
        for dnc in range(4):
            for dsub in range(8):
                slot = strip_ct["i"] % 2
                strip_ct["i"] += 1
                def mkld(dnc=dnc, dsub=dsub, slot=slot):
                    def f(e):
                        src = bass.AP(wq_bf, dsub * 256 * D + dnc * 512,
                                      [[D, 128], [128 * D, 2], [1, 512]])
                        dst = bass.AP(wst_sb, slot * 1024, [[2 * 1024, 128], [512, 2], [1, 512]])
                        return e.dma_start(out=dst, in_=src)
                    return f
                eldw = sch.add("DS", mkld(), deps=[e_wqbf, prev_strip_reader[slot]] + prev_unit)
                for dk in range(2):
                    for p in range(4):
                        kglob = dsub * 2 + dk
                        def mk(dk=dk, p=p, kglob=kglob, slot=slot):
                            def f(e):
                                return e.matmul(qps_l[p][:, 0:512],
                                                XqT2[:, kglob * 512 + p * 128:kglob * 512 + p * 128 + 128],
                                                wst_sb[:, slot * 1024 + dk * 512:slot * 1024 + (dk + 1) * 512],
                                                start=(kglob == 0), stop=(kglob == 15))
                            return f
                        dps = [eldw]
                        if kglob == 0:
                            dps += ev_XqT + bfree(qps_l[p])
                        else:
                            dps += [mm_last[p]]
                        mm_last[p] = sch.add("T", mk(), deps=dps)
                        if dk == 1 and p == 3:
                            prev_strip_reader[slot] = mm_last[3]
            for p in range(4):
                def mkev(p=p, dnc=dnc):
                    def f(e):
                        return e.copy(out=Q_sb[:, p * D + dnc * 512:p * D + (dnc + 1) * 512],
                                      in_=qps_l[p][:, 0:512])
                    return f
                eev = sch.add("A", mkev(), deps=[mm_last[p]])
                bset(qps_l[p], [eev])
                lo = bass.AP(Q_sb, p * D + dnc * 512, [[4 * D, 128], [128, 4], [1, 64]])
                hi = bass.AP(Q_sb, p * D + dnc * 512 + 64, [[4 * D, 128], [128, 4], [1, 64]])
                def cq(t, u=u, p=p):
                    return bass.AP(t, (u * 4 + p) * 64, [[16 * 64, 128], [0, 4], [1, 64]])
                q1 = sch.add("V", lambda e, lo=lo, p=p, u=u: e.tensor_tensor(
                    out=ropeT[:, 0:256], in0=lo, in1=cq(cosq, u, p), op=ALU.mult), deps=[eev] + ev_tabq)
                q2 = sch.add("V", lambda e, hi=hi, p=p, u=u: e.tensor_tensor(
                    out=ropeT[:, 256:512], in0=hi, in1=cq(sinq, u, p), op=ALU.mult), deps=[eev])
                q3 = sch.add("V", lambda e, lo=lo, p=p, u=u: e.tensor_tensor(
                    out=ropeT[:, 512:768], in0=lo, in1=cq(sinq, u, p), op=ALU.mult), deps=[eev])
                q4 = sch.add("V", lambda e, hi=hi, p=p, u=u: e.tensor_tensor(
                    out=ropeT[:, 768:1024], in0=hi, in1=cq(cosq, u, p), op=ALU.mult), deps=[eev])
                q5 = sch.add("V", lambda e, lo=lo: e.tensor_tensor(
                    out=lo, in0=ropeT[:, 0:256], in1=ropeT[:, 256:512], op=ALU.subtract), deps=[q1, q2, q3])
                q6 = sch.add("V", lambda e, hi=hi: e.tensor_tensor(
                    out=hi, in0=ropeT[:, 512:768], in1=ropeT[:, 768:1024], op=ALU.add), deps=[q4, q5])
                ev_rope_q[(p, dnc)] = q6

        last_q_mms = [prev_strip_reader[0], prev_strip_reader[1]]
        ev_QT = []
        for hpair in range(8):
            bank = talloc()
            trs = []
            for hh in range(2):
                h = hpair * 2 + hh
                dnc = h // 4
                for p in range(4):
                    def mk(h=h, p=p, hh=hh, bank=bank):
                        def f(e):
                            return e.transpose(bank[:, hh * 512 + p * 128:hh * 512 + (p + 1) * 128],
                                               Q_sb[:, p * D + h * 128:p * D + (h + 1) * 128], eye_sb[:])
                        return f
                    trs.append(sch.add("T", mk(), deps=[ev_rope_q[(p, dnc)]] + bfree(bank) + last_q_mms))
            def mkev(hpair=hpair, bank=bank):
                def f(e):
                    return e.tensor_copy(out=XqT2[:, hpair * 1024:(hpair + 1) * 1024], in_=bank[:, 0:1024])
                return f
            ev = sch.add("V", mkev(), deps=trs)
            bset(bank, [ev])
            ev_QT.append(ev)

        ev_attn_evac = {}
        ev_rd = {}
        for h in range(16):
            g = h // 4
            pvb = PBANK[h % 2]
            denom_prev = None
            pv_prev = None
            for tt in range(TT):
                sbx = tt % 3
                sbk = SBANK[sbx]
                def mkS(h=h, tt=tt, g=g, sbk=sbk):
                    def f(e):
                        return e.matmul(sbk[:, 0:512], K_sb[:, g * S + tt * 128:g * S + (tt + 1) * 128],
                                        XqT2[:, h * 512:(h + 1) * 512], start=True, stop=True)
                    return f
                eS = sch.add("T", mkS(), deps=[evK[tt]] + ev_QT + bfree(sbk))
                def mkE(sbx=sbx, sbk=sbk):
                    def f(e):
                        return e.activation(out=PT_sb[:, sbx * 512:(sbx + 1) * 512], in_=sbk[:, 0:512],
                                            func=AF.Exp, scale=consts[:, 0:1])
                    return f
                eE = sch.add("A", mkE(), deps=[eS, ec2] + slot_users[sbx])
                bset(sbk, [eE])
                slot_users[sbx] = []
                last_pt = eE
                if tt >= mstart:
                    m = tt - mstart
                    def mkM(sbx=sbx, m=m):
                        def f(e):
                            sl = slice(sbx * 512, (sbx + 1) * 512)
                            return e.tensor_tensor(out=PT_sb[:, sl], in0=PT_sb[:, sl],
                                                   in1=mask_sb[:, m * 512:(m + 1) * 512], op=ALU.mult)
                        return f
                    last_pt = sch.add("V", mkM(), deps=[eE, e_mload])
                def mkD(sbx=sbx, tt=tt, TT=TT):
                    def f(e):
                        return e.matmul(M0[0:1, 0:512], ones_sb[:, 0:1],
                                        PT_sb[:, sbx * 512:(sbx + 1) * 512],
                                        start=(tt == 0), stop=(tt == TT - 1))
                    return f
                dD = [last_pt, e_ones] + (bfree(M0) if tt == 0 else [denom_prev])
                eD = sch.add("T", mkD(), deps=dD)
                denom_prev = eD
                def mkP(tt=tt, g=g, sbx=sbx, pvb=pvb, TT=TT):
                    def f(e):
                        return e.matmul(pvb[:, 0:512], V_sb[:, tt * 512 + g * 128:tt * 512 + g * 128 + 128],
                                        PT_sb[:, sbx * 512:(sbx + 1) * 512],
                                        start=(tt == 0), stop=(tt == TT - 1))
                    return f
                dP = [last_pt, evV[tt], eD] + (bfree(pvb) if tt == 0 else [pv_prev])
                eP = sch.add("T", mkP(), deps=dP)
                pv_prev = eP
                slot_users[sbx] = [eD, eP]
            eOv = sch.add("A", lambda e, pvb=pvb: e.copy(out=outT_sb[:], in_=pvb[:, 0:512]),
                          deps=[pv_prev] + prev_outT_readers)
            eDe = sch.add("V", lambda e: e.tensor_copy(out=scr_a[0:1, 0:512], in_=M0[0:1, 0:512]),
                          deps=[denom_prev])
            bset(M0, [eDe])
            dtr = []
            for qc in range(4):
                def mkT(qc=qc, pvb=pvb):
                    def f(e):
                        return e.transpose(pvb[0:128, qc:qc + 1], scr_a[0:1, qc * 128:(qc + 1) * 128],
                                           eye32_sb[0:1, 0:1])
                    return f
                dtr.append(sch.add("T", mkT(), deps=[eDe, e_eye32, eOv]))
            erc = sch.add("V", lambda e, h=h, pvb=pvb: e.reciprocal(out=rd_all[:, h * 4:h * 4 + 4],
                                                                    in_=pvb[0:128, 0:4]), deps=dtr)
            bset(pvb, [erc])
            ev_rd[h] = erc
            bank = talloc()
            otr = []
            for qc in range(4):
                def mkO(qc=qc, bank=bank):
                    def f(e):
                        return e.transpose(bank[:, qc * 128:(qc + 1) * 128],
                                           outT_sb[:, qc * 128:(qc + 1) * 128], eye_sb[:])
                    return f
                otr.append(sch.add("T", mkO(), deps=[eOv] + bfree(bank)))
            prev_outT_readers = otr
            def mkAv(h=h, bank=bank):
                def f(e):
                    dst = bass.AP(attn_sb, h * 128, [[4 * D, 128], [D, 4], [1, 128]])
                    return e.tensor_copy(out=dst, in_=bank[:, 0:512])
                return f
            eAv = sch.add("V", mkAv(), deps=otr)
            bset(bank, [eAv])
            ev_attn_evac[h] = eAv

        ev_post = {qc: [] for qc in range(4)}
        for h in range(16):
            for qc in range(4):
                def mkp(h=h, qc=qc):
                    def f(e):
                        sl0 = qc * D + h * 128
                        return e.tensor_scalar(out=attn_sb[:, sl0:sl0 + 128], in0=attn_sb[:, sl0:sl0 + 128],
                                               scalar1=rd_all[:, h * 4 + qc:h * 4 + qc + 1],
                                               scalar2=None, op0=ALU.mult)
                    return f
                ev_post[qc].append(sch.add("V", mkp(), deps=[ev_attn_evac[h]], sdeps=[ev_rd[h]]))

        ev_XoT = []
        ev_last_dtr = dtr
        for qc in range(4):
            esq = sch.add("A", lambda e, qc=qc: e.activation(out=scr_c[:], in_=attn_sb[:, qc * D:(qc + 1) * D],
                                                             func=AF.Square, accum_out=svec[:, 0:1]),
                          deps=ev_post[qc] + (ev_XoT[-2:] if ev_XoT else []))
            er1 = sch.add("V", lambda e: e.tensor_scalar(out=svec[:, 1:2], in0=svec[:, 0:1],
                                                         scalar1=INV2048, scalar2=consts[:, 4:5],
                                                         op0=ALU.mult, op1=ALU.add), deps=[esq, ec0])
            er2 = sch.add("A", lambda e: e.activation(out=svec[:, 1:2], in_=svec[:, 1:2], func=AF.Sqrt),
                          deps=[er1])
            er3 = sch.add("V", lambda e: e.reciprocal(out=svec[:, 1:2], in_=svec[:, 1:2]), deps=[er2])
            ean = sch.add("V", lambda e, qc=qc: e.scalar_tensor_tensor(
                out=scr_a[:], in0=attn_sb[:, qc * D:(qc + 1) * D], scalar=svec[:, 1:2],
                in1=normw_sb[:], op0=ALU.mult, op1=ALU.mult),
                deps=[esq, e_normw] + (ev_last_dtr if qc == 0 else []), sdeps=[er3])
            eq2 = quant_only(ean)
            ev_XoT += transpose_tile(scr_c, 0, eq2, Q_sb, lambda k, qc=qc: k * 512 + qc * 128)

        wps_l = [S0, S1, S2, P0]
        wo_last = {}
        for dnc in range(4):
            for dsub in range(8):
                slot = strip_ct["i"] % 2
                strip_ct["i"] += 1
                def mkld(dnc=dnc, dsub=dsub, slot=slot):
                    def f(e):
                        src = bass.AP(wo_bf, dsub * 256 * D + dnc * 512,
                                      [[D, 128], [128 * D, 2], [1, 512]])
                        dst = bass.AP(wst_sb, slot * 1024, [[2 * 1024, 128], [512, 2], [1, 512]])
                        return e.dma_start(out=dst, in_=src)
                    return f
                eldw = sch.add("DS", mkld(), deps=[e_wobf, prev_strip_reader[slot]])
                for dk in range(2):
                    for qc in range(4):
                        kglob = dsub * 2 + dk
                        def mk(dk=dk, qc=qc, kglob=kglob, slot=slot):
                            def f(e):
                                return e.matmul(wps_l[qc][:, 0:512],
                                                Q_sb[:, kglob * 512 + qc * 128:kglob * 512 + qc * 128 + 128],
                                                wst_sb[:, slot * 1024 + dk * 512:slot * 1024 + (dk + 1) * 512],
                                                start=(kglob == 0), stop=(kglob == 15))
                            return f
                        dps = [eldw]
                        if kglob == 0:
                            dps += ev_XoT + bfree(wps_l[qc])
                        else:
                            dps += [wo_last[qc]]
                        wo_last[qc] = sch.add("T", mk(), deps=dps)
                        if dk == 1 and qc == 3:
                            prev_strip_reader[slot] = wo_last[3]
            for qc in range(4):
                slot8 = (qc * 4 + dnc) % 8
                def mkev(qc=qc, dnc=dnc, slot8=slot8):
                    def f(e):
                        return e.tensor_tensor(out=out_sb[:, slot8 * 512:(slot8 + 1) * 512],
                                               in0=wps_l[qc][:, 0:512],
                                               in1=xn_sb[:, qc * D + dnc * 512:qc * D + (dnc + 1) * 512],
                                               op=ALU.add)
                    return f
                dstore = [prev_out_store.get(slot8)]
                eev = sch.add("V", mkev(), deps=[wo_last[qc], ev_ffn_done[qc]] + dstore)
                bset(wps_l[qc], [eev])
                def mkst(qc=qc, u=u, dnc=dnc, slot8=slot8):
                    def f(e):
                        row0 = u * 512 + qc * 128
                        dst = bass.AP(out_h, row0 * D + dnc * 512, [[D, 128], [1, 512]])
                        return e.dma_start(out=dst, in_=out_sb[:, slot8 * 512:(slot8 + 1) * 512])
                    return f
                est = sch.add("DG", mkst(), deps=[eev])
                prev_out_store[slot8] = est
                ev_stores.append(est)
        prev_unit = [sch.last("T"), sch.last("V"), sch.last("A"), ev_stores[-1]]

    sch.add("DS", lambda e: e.nop(), deps=ev_stores)

    sch.finalize()
    sems = {}
    semstack = contextlib.ExitStack()
    for c in Sch.CH:
        sems[c] = semstack.enter_context(nc.semaphore("sem" + c))
    with nc.Block() as block:
        @block.tensor
        def _(eng):
            sch.emit("T", eng, sems)

        @block.vector
        def _(eng):
            sch.emit("V", eng, sems)

        @block.scalar
        def _(eng):
            sch.emit("A", eng, sems)

        @block.sync
        def _(eng):
            sch.emit("DS", eng, sems)

        @block.gpsimd
        def _(eng):
            sch.emit("DG", eng, sems)
    semstack.close()
    stack.close()
    return nc


_NC = None


def _get_nc():
    global _NC
    if _NC is None:
        _NC = build_nc()
    return _NC


def kernel(L0_attn_x, L0_attn_cache_k, L0_attn_cache_v, L0_attn_wqkv, L0_attn_wo, L0_attn_norm_w,
           L0_ffn_select_weight, L0_ffn_select_mean_diff, L0_ffn_imp_center, L0_ffn_imp_slope,
           L0_ffn_lora_update, L0_ffn_lora_degen, L0_ffn_bias_w, L0_ffn_debias_w):
    x = np.asarray(L0_attn_x, dtype=np.float32)
    wqkv = np.asarray(L0_attn_wqkv, dtype=np.float32)
    wo = np.asarray(L0_attn_wo, dtype=np.float32)
    norm_w = np.asarray(L0_attn_norm_w, dtype=np.float32)
    selw = np.asarray(L0_ffn_select_weight, dtype=np.float32)
    mud = np.asarray(L0_ffn_select_mean_diff, dtype=np.float32)
    impc = np.asarray(L0_ffn_imp_center, dtype=np.float32)
    imps = np.asarray(L0_ffn_imp_slope, dtype=np.float32)
    lup = np.asarray(L0_ffn_lora_update, dtype=np.float32)
    ldg = np.asarray(L0_ffn_lora_degen, dtype=np.float32)
    biw = np.asarray(L0_ffn_bias_w, dtype=np.float32)
    dbw = np.asarray(L0_ffn_debias_w, dtype=np.float32)

    B = x.shape[0]

    perm128 = np.concatenate([np.arange(0, 128, 2), np.arange(1, 128, 2)])
    wqkvT = np.ascontiguousarray(wqkv.T)
    cols = np.arange(3072)
    for blk in range(20):
        base = blk * 128
        if base >= 2560:
            break
        cols[base:base + 128] = base + perm128
    wqkvT = wqkvT[:, cols]
    wqT_a = np.ascontiguousarray(wqkvT[:, 0:2048])
    wkvT_a = np.ascontiguousarray(wqkvT[:, 2048:3072])
    woT_a = np.ascontiguousarray(wo.T)

    normw_a = np.ascontiguousarray(np.broadcast_to(norm_w[None, :], (128, D)))
    eye_a = np.eye(128, dtype=ml_dtypes.bfloat16)
    eye32_a = np.eye(128, dtype=np.float32)
    tiles = np.arange(NPT1)
    pp = np.arange(128)
    posk_a = np.ascontiguousarray(
        np.broadcast_to((tiles[None, :, None] * 128 + pp[:, None, None]).astype(np.float32),
                        (128, NPT1, 64)).reshape(128, NPT1 * 64))
    jrow_a = np.ascontiguousarray(np.broadcast_to(np.arange(64, dtype=np.float32)[None, :], (128, 64)))
    wselT_a = np.ascontiguousarray(selw.transpose(1, 0, 2).reshape(D, 64))
    updT_a = np.ascontiguousarray(lup.transpose(1, 2, 0).reshape(D, 64))
    w2aug_a = np.concatenate([ldg.transpose(2, 0, 1).reshape(64, D), biw], axis=0)
    w2aug_a = np.ascontiguousarray(w2aug_a)
    mudT_a = np.ascontiguousarray(mud.T)
    debT_a = np.ascontiguousarray(dbw.T)
    slopeb_a = np.ascontiguousarray(np.broadcast_to(imps[None, :], (128, 8)))
    centb_a = np.ascontiguousarray(np.broadcast_to(impc[None, :], (128, 8)))
    nvals_a = np.ascontiguousarray(np.broadcast_to(np.arange(8, dtype=np.float32)[None, :], (128, 8)))
    nmB_a = np.ascontiguousarray(nvals_a - 1000.0)
    nn = np.arange(8)
    diagns_a = np.ascontiguousarray(
        (np.repeat(nn, 8)[:, None] == nn[None, :]).astype(ml_dtypes.bfloat16))
    diaghn_a = np.ascontiguousarray(
        (np.tile(nn, 8)[:, None] == nn[None, :]).astype(ml_dtypes.bfloat16))

    nc = _get_nc()

    in_maps = []
    for c in range(NCORE := 8):
        b = c // 2
        par = c % 2
        chunks = CHUNKS[par]
        rows = np.concatenate([np.arange(ch * 512, (ch + 1) * 512) for ch in chunks])
        xq_own_a = np.ascontiguousarray(x[b][rows])
        posq_a = np.ascontiguousarray(
            np.broadcast_to(
                (rows.reshape(16, 128).T[:, :, None]).astype(np.float32), (128, 16, 64)
            ).reshape(128, 16 * 64))
        mk = np.zeros((UNITS, 8, 128, 512), dtype=ml_dtypes.bfloat16)
        for j in range(UNITS):
            q0 = chunks[j] * 512
            for m in range(8):
                t0 = RUN[j] - 1024 + m * 128
                tloc = t0 + np.arange(128)
                qloc = q0 + np.arange(512)
                mk[j, m] = (tloc[:, None] <= qloc[None, :]).astype(ml_dtypes.bfloat16)
        im = {
            "xin": np.ascontiguousarray(x[b]),
            "xq_own": xq_own_a,
            "wqT": wqT_a, "wkvT": wkvT_a, "woT": woT_a,
            "normw": normw_a, "eye": eye_a, "eye32": eye32_a,
            "posk": posk_a, "posq": posq_a, "jrow": jrow_a,
            "masks": np.ascontiguousarray(mk.reshape(UNITS * 8 * 128, 512)),
            "wselT": wselT_a, "updT": updT_a, "w2aug": w2aug_a,
            "mudT": mudT_a, "debT": debT_a,
            "slopeb": slopeb_a, "centb": centb_a, "nvalsb": nvals_a, "nmBb": nmB_a,
            "diagns": diagns_a, "diaghn": diaghn_a,
        }
        in_maps.append(im)

    res = run_bass_kernel_spmd(nc, in_maps, core_ids=list(range(8)))
    out = np.empty((B, S, D), dtype=np.float32)
    for c in range(8):
        b = c // 2
        par = c % 2
        chunks = CHUNKS[par]
        o = res.results[c]["out"]
        for j, ch in enumerate(chunks):
            out[b, ch * 512:(ch + 1) * 512] = o[j * 512:(j + 1) * 512]
    return out


# version 03
# speedup vs baseline: 1.1712x; 1.1712x over previous
import numpy as np
import ml_dtypes
import concourse.bass as bass
import concourse.mybir as mybir
from concourse.bass_utils import run_bass_kernel_spmd
from concourse.alu_op_type import AluOpType as ALU

F32 = mybir.dt.float32
BF16 = mybir.dt.bfloat16
FP16 = mybir.dt.float16
I32 = mybir.dt.int32
AF = mybir.ActivationFunctionType
AX = mybir.AxisListType

D = 2048
S = 4096
NPT1 = 32
UNITS = 4
RUN = [1024, 2048, 3072, 4096]
CHUNKS = [[0, 3, 4, 7], [1, 2, 5, 6]]
EPS = 1e-5
INV2048 = 1.0 / 2048.0


class Sch:
    CH = ("T", "V", "A", "DS", "DG")

    def __init__(self):
        self.ops = {c: [] for c in self.CH}

    def add(self, ch, fn, deps=(), sdeps=()):
        deps = [d for d in deps if d is not None]
        sdeps = [d for d in sdeps if d is not None]
        idx = len(self.ops[ch])
        self.ops[ch].append({"fn": fn, "deps": deps, "sdeps": sdeps})
        return (ch, idx)

    def last(self, ch):
        return (ch, len(self.ops[ch]) - 1) if self.ops[ch] else None

    def finalize(self):
        self.marked = {c: set() for c in self.CH}
        for c in ("DS", "DG"):
            self.marked[c] = set(range(len(self.ops[c])))
        for c in self.CH:
            for op in self.ops[c]:
                for (dc, di) in op["deps"] + op["sdeps"]:
                    if dc != c and dc not in ("DS", "DG"):
                        self.marked[dc].add(di)
        self.incord = {}
        for c in self.CH:
            cnt, m = 0, []
            for i in range(len(self.ops[c])):
                if i in self.marked[c]:
                    cnt += 1
                m.append(cnt)
            self.incord[c] = m

    NSLOT = 4

    def emit(self, ch, eng, sems):
        watermark = {}
        last_barrier = -1
        isdma = ch in ("DS", "DG")

        def wait_for(dc, di):
            if dc in ("DS", "DG"):
                slot = di % Sch.NSLOT
                key = (dc, slot)
                tgt = 16 * (di // Sch.NSLOT + 1)
                sem = sems[dc][slot]
            else:
                key = dc
                tgt = self.incord[dc][di]
                sem = sems[dc]
            if tgt > watermark.get(key, 0):
                eng.wait_ge(sem, tgt)
                watermark[key] = tgt

        for i, op in enumerate(self.ops[ch]):
            need_drain = False
            for (dc, di) in op["sdeps"]:
                if dc == ch and not isdma:
                    if di > last_barrier and di >= i - 12:
                        need_drain = True
                else:
                    wait_for(dc, di)
            for (dc, di) in op["deps"]:
                if dc == ch and not isdma:
                    if di > last_barrier and di >= i - 1:
                        need_drain = True
                else:
                    wait_for(dc, di)
            if need_drain:
                eng.drain()
                last_barrier = i - 1
            if isdma:
                slot = i % Sch.NSLOT
                if i >= Sch.NSLOT:
                    prev_uses = (i - Sch.NSLOT) // Sch.NSLOT + 1
                    tgt = 16 * prev_uses
                    key = (ch, slot)
                    if tgt > watermark.get(key, 0):
                        eng.wait_ge(sems[ch][slot], tgt)
                        watermark[key] = tgt
            ins = op["fn"](eng)
            if isdma:
                if ins is not None:
                    ins.then_inc(sems[ch][i % Sch.NSLOT], 16)
            elif i in self.marked[ch] and ins is not None:
                ins.then_inc(sems[ch], 1)


def build_nc():
    import contextlib
    nc = bass.Bass("TRN2", target_bir_lowering=False)
    sch = Sch()
    stack = contextlib.ExitStack()
    stack1 = contextlib.ExitStack()

    xin = nc.dram_tensor("xin", [S, D], F32, kind="ExternalInput")
    xq_own = nc.dram_tensor("xq_own", [2048, D], F32, kind="ExternalInput")
    wqT = nc.dram_tensor("wqT", [D, D], F32, kind="ExternalInput")
    wkvT = nc.dram_tensor("wkvT", [D, 1024], F32, kind="ExternalInput")
    woT = nc.dram_tensor("woT", [D, D], F32, kind="ExternalInput")
    normw_h = nc.dram_tensor("normw", [128, D], F32, kind="ExternalInput")
    eye_h = nc.dram_tensor("eye", [128, 128], BF16, kind="ExternalInput")
    eye32_h = nc.dram_tensor("eye32", [128, 128], F32, kind="ExternalInput")
    posk_h = nc.dram_tensor("posk", [128, NPT1 * 64], F32, kind="ExternalInput")
    posq_h = nc.dram_tensor("posq", [128, 16 * 64], F32, kind="ExternalInput")
    jrow_h = nc.dram_tensor("jrow", [128, 64], F32, kind="ExternalInput")
    masks_h = nc.dram_tensor("masks", [UNITS * 8 * 128, 512], BF16, kind="ExternalInput")
    wselT_h = nc.dram_tensor("wselT", [D, 64], F32, kind="ExternalInput")
    updT_h = nc.dram_tensor("updT", [D, 64], F32, kind="ExternalInput")
    w2aug_h = nc.dram_tensor("w2aug", [72, D], F32, kind="ExternalInput")
    mudT_h = nc.dram_tensor("mudT", [D, 8], F32, kind="ExternalInput")
    debT_h = nc.dram_tensor("debT", [D, 8], F32, kind="ExternalInput")
    slopeb_h = nc.dram_tensor("slopeb", [128, 8], F32, kind="ExternalInput")
    centb_h = nc.dram_tensor("centb", [128, 8], F32, kind="ExternalInput")
    nvalsb_h = nc.dram_tensor("nvalsb", [128, 8], F32, kind="ExternalInput")
    nmBb_h = nc.dram_tensor("nmBb", [128, 8], F32, kind="ExternalInput")
    diagns_h = nc.dram_tensor("diagns", [64, 8], BF16, kind="ExternalInput")
    diaghn_h = nc.dram_tensor("diaghn", [64, 8], BF16, kind="ExternalInput")
    out_h = nc.dram_tensor("out", [2048, D], F32, kind="ExternalOutput")
    wq_bf = nc.dram_tensor("wq_bf", [D, D], BF16)
    wo_bf = nc.dram_tensor("wo_bf", [D, D], BF16)

    def sb(name, shape, dt):
        return stack.enter_context(nc.sbuf_tensor(name, shape, dt))

    def sb1(name, shape, dt):
        return stack1.enter_context(nc.sbuf_tensor(name, shape, dt))

    def psum(name, shape, dt):
        return stack.enter_context(nc.psum_tensor(name, shape, dt))

    K_sb = sb("K_sb", [128, 4 * S], BF16)
    V_sb = sb("V_sb", [128, NPT1 * 512], BF16)
    cosq = sb("cosq", [128, 16 * 64], BF16)
    sinq = sb("sinq", [128, 16 * 64], BF16)
    eye_sb = sb("eye_sb", [128, 128], BF16)
    eye32_sb = sb("eye32_sb", [128, 128], F32)
    ones_sb = sb("ones_sb", [128, 1], BF16)
    ones_row = sb("ones_row", [1, 128], BF16)
    normw_sb = sb("normw_sb", [128, D], BF16)
    ffnW = sb("ffnW", [128, 16 * 136], BF16)
    debT_sb = sb("debT_sb", [128, 16 * 8], BF16)
    w2aug_sb = sb("w2aug_sb", [128, D], BF16)
    cselb = sb("cselb", [128, 64], F32)
    cupdb = sb("cupdb", [128, 64], F32)
    musqb = sb("musqb", [128, 8], F32)
    slopeb_sb = sb("slopeb_sb", [128, 8], F32)
    centb_sb = sb("centb_sb", [128, 8], F32)
    nvalsb_sb = sb("nvalsb_sb", [128, 8], F32)
    nmBb_sb = sb("nmBb_sb", [128, 8], F32)
    diagns_sb = sb("diagns_sb", [64, 8], BF16)
    diaghn_sb = sb("diaghn_sb", [64, 8], BF16)
    consts = sb("consts", [128, 8], F32)
    scr_a = sb("scr_a", [128, D], F32)
    scr_i = sb("scr_i", [128, D // 2], I32)
    scr_c = sb("scr_c", [128, D], BF16)
    ropeT = sb("ropeT", [128, 1024], BF16)
    stats = sb("stats", [128, 32], F32)
    svec = sb("svec", [128, 16], F32)
    gate_s = sb("gate_s", [128, 128], F32)
    rd_all = sb("rd_all", [128, 64], F32)
    zaugT_sb = sb("zaugT_sb", [128, 128], BF16)
    outT_sb = sb("outT_sb", [128, 512], BF16)

    S0 = psum("S0", [128, 512], F32)
    S1 = psum("S1", [128, 512], F32)
    S2 = psum("S2", [128, 512], F32)
    P0 = psum("P0", [128, 512], F32)
    P1 = psum("P1", [128, 512], F32)
    T0 = psum("T0", [128, 1024], BF16)
    T1 = psum("T1", [128, 1024], BF16)
    M0 = psum("M0", [128, 512], F32)
    SBANK = [S0, S1, S2]
    PBANK = [P0, P1]
    TBANK = [T0, T1]

    bank_free = {}

    def bfree(b):
        return list(bank_free.get(id(b), []))

    def bset(b, evs):
        bank_free[id(b)] = [e for e in evs if e is not None]

    tstate = {"i": 0}

    def talloc():
        b = TBANK[tstate["i"] % 2]
        tstate["i"] += 1
        return b

    e_eye = sch.add("DS", lambda e: e.dma_start(out=eye_sb[:], in_=eye_h.ap()))
    e_eye32 = sch.add("DS", lambda e: e.dma_start(out=eye32_sb[:], in_=eye32_h.ap()))
    e_slope = sch.add("DS", lambda e: e.dma_start(out=slopeb_sb[:], in_=slopeb_h.ap()))
    e_cent = sch.add("DS", lambda e: e.dma_start(out=centb_sb[:], in_=centb_h.ap()))
    e_nvals = sch.add("DS", lambda e: e.dma_start(out=nvalsb_sb[:], in_=nvalsb_h.ap()))
    e_nmB = sch.add("DS", lambda e: e.dma_start(out=nmBb_sb[:], in_=nmBb_h.ap()))
    e_dns = sch.add("DS", lambda e: e.dma_start(out=diagns_sb[:], in_=diagns_h.ap()))
    e_dhn = sch.add("DS", lambda e: e.dma_start(out=diaghn_sb[:], in_=diaghn_h.ap()))
    e_normw = sch.add("DG", lambda e: e.dma_start(out=normw_sb[:], in_=normw_h.ap()))

    def _hs(t, pstep, kstep, inner):
        return bass.AP(t, 0, [[pstep, 128], [kstep, 16], [1, inner]])

    e_fw = []
    e_fw.append(sch.add("DG", lambda e: e.dma_start(
        out=bass.AP(ffnW, 0, [[16 * 136, 128], [136, 16], [1, 64]]), in_=_hs(wselT_h, 64, 64 * 128, 64))))
    e_fw.append(sch.add("DG", lambda e: e.dma_start(
        out=bass.AP(ffnW, 64, [[16 * 136, 128], [136, 16], [1, 64]]), in_=_hs(updT_h, 64, 64 * 128, 64))))
    e_fw.append(sch.add("DG", lambda e: e.dma_start(
        out=bass.AP(ffnW, 128, [[16 * 136, 128], [136, 16], [1, 8]]), in_=_hs(mudT_h, 8, 8 * 128, 8))))
    e_fw.append(sch.add("DG", lambda e: e.dma_start(
        out=bass.AP(debT_sb, 0, [[16 * 8, 128], [8, 16], [1, 8]]), in_=_hs(debT_h, 8, 8 * 128, 8))))
    e_fw.append(sch.add("DG", lambda e: e.dma_start(out=w2aug_sb[0:72, :], in_=w2aug_h.ap())))
    e_wqbf = sch.add("DG", lambda e: e.dma_start(out=wq_bf.ap(), in_=wqT.ap()))
    e_wobf = sch.add("DG", lambda e: e.dma_start(out=wo_bf.ap(), in_=woT.ap()))

    def _mkc(e):
        e.memset(consts[:, 0:1], 128.0)
        e.memset(consts[:, 1:2], 127.0)
        e.memset(consts[:, 2:3], 1.0)
        e.memset(consts[:, 4:5], EPS)
        e.memset(consts[:, 6:7], 500000.0)
        return e.memset(consts[:, 7:8], 0.0)
    ec0 = sch.add("V", _mkc)
    ec1 = sch.add("A", lambda e: e.activation(out=consts[:, 0:1], in_=consts[:, 7:8], func=AF.Sqrt,
                                              scale=0.0, bias=consts[:, 0:1]), deps=[ec0])
    ec2 = sch.add("V", lambda e: e.reciprocal(out=consts[:, 0:1], in_=consts[:, 0:1]), deps=[ec1])
    ec3 = sch.add("V", lambda e: e.reciprocal(out=consts[:, 1:2], in_=consts[:, 1:2]), deps=[ec0, ec2])
    ec4 = sch.add("A", lambda e: e.activation(out=consts[:, 5:6], in_=consts[:, 7:8], func=AF.Arctan,
                                              scale=0.0, bias=consts[:, 2:3]), deps=[ec0, ec3])
    ec5 = sch.add("V", lambda e: e.tensor_scalar(out=consts[:, 2:3], in0=consts[:, 5:6], scalar1=8.0,
                                                 scalar2=None, op0=ALU.mult), deps=[ec4])
    ec6 = sch.add("V", lambda e: e.tensor_scalar(out=consts[:, 5:6], in0=consts[:, 5:6], scalar1=2.0,
                                                 scalar2=None, op0=ALU.mult), deps=[ec4, ec5])
    ec7 = sch.add("V", lambda e: e.reciprocal(out=consts[:, 3:4], in_=consts[:, 2:3]), deps=[ec5, ec6])
    ec8 = sch.add("A", lambda e: e.activation(out=consts[:, 6:7], in_=consts[:, 7:8], func=AF.Ln,
                                              scale=0.0, bias=consts[:, 6:7]), deps=[ec7, ec0])
    e_ones = sch.add("V", lambda e: e.memset(ones_sb[:], 1.0), deps=[ec0])
    e_ones2 = sch.add("V", lambda e: e.memset(ones_row[:], 1.0), deps=[ec0])

    cosk = sb1("cosk", [128, NPT1 * 64], BF16)
    sink = sb1("sink", [128, NPT1 * 64], BF16)
    tabw = sb1("tabw", [128, NPT1 * 64], F32)
    tabw2 = sb1("tabw2", [128, NPT1 * 64], F32)
    tabw3 = sb1("tabw3", [128, NPT1 * 64], I32)
    freqb = sb1("freqb", [128, 64], F32)
    e_posk = sch.add("DS", lambda e: e.dma_start(out=tabw[:], in_=posk_h.ap()))
    e_jrow = sch.add("DS", lambda e: e.dma_start(out=freqb[:], in_=jrow_h.ap()))
    ef0 = sch.add("V", lambda e: e.tensor_scalar(out=freqb[:], in0=freqb[:], scalar1=consts[:, 6:7],
                                                 scalar2=-0.015625, op0=ALU.mult, op1=ALU.mult),
                  deps=[e_jrow, ec8])
    ef1 = sch.add("A", lambda e: e.activation(out=freqb[:], in_=freqb[:], func=AF.Exp), deps=[ef0])

    def build_tab(width, cos_dst, sin_dst, dep_pos):
        e0 = sch.add("V", lambda e: e.tensor_tensor(
            out=tabw2[:, 0:width], in0=tabw[:, 0:width],
            in1=bass.AP(freqb, 0, [[64, 128], [0, width // 64], [1, 64]]), op=ALU.mult),
            deps=[dep_pos, ef1])
        out = []
        for which, dst in (("sin", sin_dst), ("cos", cos_dst)):
            if which == "cos":
                base = sch.add("V", lambda e: e.tensor_scalar(out=tabw2[:, 0:width], in0=tabw2[:, 0:width],
                                                              scalar1=consts[:, 5:6], scalar2=None,
                                                              op0=ALU.add), deps=[e0] + out, sdeps=[ec6])
            else:
                base = e0
            e1 = sch.add("V", lambda e: e.tensor_scalar(out=tabw[:, 0:width], in0=tabw2[:, 0:width],
                                                        scalar1=consts[:, 3:4], scalar2=None,
                                                        op0=ALU.mult), deps=[base], sdeps=[ec7])
            e2 = sch.add("V", lambda e: e.tensor_copy(out=tabw3[:, 0:width], in_=tabw[:, 0:width]), deps=[e1])
            e3 = sch.add("V", lambda e: e.tensor_scalar(out=tabw[:, 0:width], in0=tabw3[:, 0:width],
                                                        scalar1=consts[:, 2:3], scalar2=None,
                                                        op0=ALU.mult), deps=[e2], sdeps=[ec5])
            e4 = sch.add("V", lambda e: e.tensor_tensor(out=tabw[:, 0:width], in0=tabw2[:, 0:width],
                                                        in1=tabw[:, 0:width], op=ALU.subtract), deps=[e3])
            e5 = sch.add("A", lambda e, dst=dst: e.activation(out=dst[:, 0:width], in_=tabw[:, 0:width],
                                                              func=AF.Sin), deps=[e4])
            out.append(e5)
        return out
    ev_tabk = build_tab(NPT1 * 64, cosk, sink, e_posk)
    e_posq = sch.add("DS", lambda e: e.dma_start(out=tabw[:, 0:1024], in_=posq_h.ap()), deps=ev_tabk)
    ev_tabq = build_tab(1024, cosq, sinq, e_posq)

    def diag_const(lhs_off, lhs_w, rhs_ap, diag_sb, dstb, deps):
        evs = []
        for c in range(16):
            def mk(c=c):
                def f(e):
                    return e.matmul(M0[0:lhs_w, 0:8],
                                    ffnW[:, c * 136 + lhs_off:c * 136 + lhs_off + lhs_w],
                                    rhs_ap(c),
                                    start=(c == 0), stop=(c == 15))
                return f
            evs.append(sch.add("T", mk(), deps=(deps + bfree(M0)) if c == 0 else [evs[-1]]))
        em = sch.add("V", lambda e: e.tensor_tensor(out=gate_s[0:lhs_w, 0:8], in0=M0[0:lhs_w, 0:8],
                                                    in1=diag_sb[0:lhs_w, :], op=ALU.mult),
                     deps=[evs[-1], e_dns, e_dhn])
        er = sch.add("V", lambda e: e.tensor_reduce(out=gate_s[0:lhs_w, 8:9], in_=gate_s[0:lhs_w, 0:8],
                                                    axis=AX.X, op=ALU.add), deps=[em])
        ecst = sch.add("V", lambda e: e.tensor_copy(out=scr_c[0:lhs_w, 0:1], in_=gate_s[0:lhs_w, 8:9]),
                       deps=[er])
        bank = talloc()
        etr = sch.add("T", lambda e, bank=bank: e.transpose(bank[0:1, 0:lhs_w], scr_c[0:lhs_w, 0:1],
                                                            eye_sb[0:lhs_w, 0:lhs_w]),
                      deps=[ecst, e_eye, em] + bfree(bank))
        ecp = sch.add("V", lambda e, bank=bank: e.tensor_copy(out=scr_c[0:1, 256:256 + lhs_w],
                                                              in_=bank[0:1, 0:lhs_w]), deps=[etr])
        bset(bank, [ecp])
        ebc = sch.add("T", lambda e: e.matmul(M0[0:128, 0:lhs_w], ones_row[0:1, 0:128],
                                              scr_c[0:1, 256:256 + lhs_w], start=True, stop=True),
                      deps=[ecp, e_ones2, em])
        eev = sch.add("V", lambda e: e.tensor_copy(out=dstb[:, 0:lhs_w], in_=M0[0:128, 0:lhs_w]), deps=[ebc])
        bset(M0, [eev])
        return eev
    depw = e_fw + [ev_tabk[1], ev_tabq[1]]
    e_csel = diag_const(0, 64, lambda c: ffnW[:, c * 136 + 128:c * 136 + 136], diagns_sb, cselb, depw)
    e_cupd = diag_const(64, 64, lambda c: debT_sb[:, c * 8:(c + 1) * 8], diaghn_sb, cupdb, depw + [e_csel])
    e_musq = diag_const(128, 8, lambda c: ffnW[:, c * 136 + 128:c * 136 + 136], diaghn_sb, musqb,
                        depw + [e_cupd])
    ev_setup = [e_musq]

    def ln_quant(dep_ld, x_ap, xnbf_dst, extra):
        es = []
        for c in range(4):
            es.append(sch.add("V", lambda e, c=c: e.bn_stats(out=stats[:, c * 6:(c + 1) * 6],
                                                             in_=x_ap[:, c * 512:(c + 1) * 512]),
                              deps=[dep_ld] + list(extra)))
        eag = sch.add("V", lambda e: e.bn_aggr(out=svec[:, 0:2], in_=stats[:, 0:24]), deps=es)
        ev1 = sch.add("V", lambda e: e.tensor_scalar(out=svec[:, 2:3], in0=svec[:, 1:2],
                                                     scalar1=consts[:, 4:5], scalar2=None, op0=ALU.add),
                      deps=[eag], sdeps=[ec0])
        ev2 = sch.add("A", lambda e: e.activation(out=svec[:, 2:3], in_=svec[:, 2:3], func=AF.Sqrt),
                      deps=[ev1])
        ev3 = sch.add("V", lambda e: e.reciprocal(out=svec[:, 2:3], in_=svec[:, 2:3]), deps=[ev2])
        exn = sch.add("V", lambda e: e.tensor_scalar(out=scr_a[:], in0=x_ap[:], scalar1=svec[:, 0:1],
                                                     scalar2=svec[:, 2:3], op0=ALU.subtract, op1=ALU.mult),
                      deps=[eag], sdeps=[ev3, eag])
        exnb = None
        if xnbf_dst is not None:
            exnb = sch.add("V", lambda e: e.tensor_copy(out=xnbf_dst, in_=scr_a[:]), deps=[exn])
        em0 = sch.add("V", lambda e: e.reduce_max(out=svec[:, 3:4], in_=scr_a[:], axis=AX.X,
                                                  apply_absolute_value=True), deps=[exn])
        em1 = sch.add("V", lambda e: e.tensor_scalar(out=svec[:, 3:4], in0=svec[:, 3:4], scalar1=EPS,
                                                     scalar2=None, op0=ALU.max), deps=[em0])
        em2 = sch.add("V", lambda e: e.reciprocal(out=svec[:, 4:5], in_=svec[:, 3:4]), deps=[em1])
        em3 = sch.add("V", lambda e: e.tensor_scalar(out=svec[:, 4:5], in0=svec[:, 4:5], scalar1=127.0,
                                                     scalar2=None, op0=ALU.mult), deps=[em2])
        ey = sch.add("V", lambda e: e.tensor_scalar(out=scr_a[:], in0=scr_a[:], scalar1=svec[:, 4:5],
                                                    scalar2=None, op0=ALU.mult),
                     deps=[em0] + ([exnb] if exnb else []), sdeps=[em3])
        eq = None
        for hh in range(2):
            ei = sch.add("V", lambda e, hh=hh: e.tensor_copy(out=scr_i[:],
                                                             in_=scr_a[:, hh * 1024:(hh + 1) * 1024]),
                         deps=[ey] + ([eq] if eq else []))
            eq = sch.add("V", lambda e, hh=hh: e.tensor_scalar(
                out=scr_c[:, hh * 1024:(hh + 1) * 1024], in0=scr_i[:], scalar1=svec[:, 3:4],
                scalar2=consts[:, 1:2], op0=ALU.mult, op1=ALU.mult), deps=[ei], sdeps=[em1, ec3])
        return eq, exnb

    def quant_only(dep_src):
        em0 = sch.add("V", lambda e: e.reduce_max(out=svec[:, 3:4], in_=scr_a[:], axis=AX.X,
                                                  apply_absolute_value=True), deps=[dep_src])
        em1 = sch.add("V", lambda e: e.tensor_scalar(out=svec[:, 3:4], in0=svec[:, 3:4], scalar1=EPS,
                                                     scalar2=None, op0=ALU.max), deps=[em0])
        em2 = sch.add("V", lambda e: e.reciprocal(out=svec[:, 4:5], in_=svec[:, 3:4]), deps=[em1])
        em3 = sch.add("V", lambda e: e.tensor_scalar(out=svec[:, 4:5], in0=svec[:, 4:5], scalar1=127.0,
                                                     scalar2=None, op0=ALU.mult), deps=[em2])
        ey = sch.add("V", lambda e: e.tensor_scalar(out=scr_a[:], in0=scr_a[:], scalar1=svec[:, 4:5],
                                                    scalar2=None, op0=ALU.mult), deps=[em0], sdeps=[em3])
        eq = None
        for hh in range(2):
            ei = sch.add("V", lambda e, hh=hh: e.tensor_copy(out=scr_i[:],
                                                             in_=scr_a[:, hh * 1024:(hh + 1) * 1024]),
                         deps=[ey] + ([eq] if eq else []))
            eq = sch.add("V", lambda e, hh=hh: e.tensor_scalar(
                out=scr_c[:, hh * 1024:(hh + 1) * 1024], in0=scr_i[:], scalar1=svec[:, 3:4],
                scalar2=consts[:, 1:2], op0=ALU.mult, op1=ALU.mult), deps=[ei], sdeps=[em1, ec3])
        return eq

    def transpose_tile(src_t, src_col0, src_dep, dst_t, col_of, extra=()):
        out_evs = []
        for half in range(2):
            bank = talloc()
            trs = []
            for kk in range(8):
                k = half * 8 + kk
                def mk(k=k, kk=kk, bank=bank):
                    def f(e):
                        return e.transpose(bank[:, kk * 128:(kk + 1) * 128],
                                           src_t[:, src_col0 + k * 128:src_col0 + (k + 1) * 128], eye_sb[:])
                    return f
                trs.append(sch.add("T", mk(), deps=[src_dep, e_eye] + bfree(bank) + list(extra)))
            stride = col_of(half * 8 + 1) - col_of(half * 8)
            def mkev(half=half, bank=bank, stride=stride):
                def f(e):
                    dst_ap = bass.AP(dst_t, col_of(half * 8),
                                     [[dst_t.shape[1], 128], [stride, 8], [1, 128]])
                    return e.tensor_copy(out=dst_ap, in_=bank[:, 0:1024])
                return f
            ev = sch.add("V", mkev(), deps=trs)
            bset(bank, [ev])
            out_evs.append(ev)
        return out_evs

    wkv_sb = sb1("wkv_sb", [128, 16 * 1024], BF16)
    x1_sb = sb1("x1_sb", [128, 2 * D], F32)
    XqT1 = sb1("XqT1", [128, 16 * 512], BF16)
    ktmp = sb1("ktmp", [128, 512], BF16)

    e_wkv = sch.add("DG", lambda e: e.dma_start(
        out=bass.AP(wkv_sb, 0, [[16 * 1024, 128], [1024, 16], [1, 1024]]),
        in_=bass.AP(wkvT, 0, [[1024, 128], [128 * 1024, 16], [1, 1024]])))
    evK = [None] * NPT1
    evV = [None] * NPT1
    prev_xq_users = []
    prev_x_reader = {0: None, 1: None}
    prev_ktmp_readers = []

    for t1 in range(8):
        xq_ready = []
        for p in range(4):
            pt = t1 * 4 + p
            buf = pt % 2
            x_ap = x1_sb[:, buf * D:(buf + 1) * D]
            eld = sch.add("DS", lambda e, pt=pt, x_ap=x_ap: e.dma_start(
                out=x_ap, in_=xin.ap()[pt * 128:(pt + 1) * 128, :]), deps=[prev_x_reader[buf]])
            extra = list(prev_xq_users) + (ev_setup if pt == 0 else [])
            eq, _ = ln_quant(eld, x_ap, None, extra)
            prev_x_reader[buf] = eq
            xq_ready += transpose_tile(scr_c, 0, eq, XqT1, lambda k, p=p: k * 512 + p * 128,
                                       extra=prev_xq_users)
        prev_xq_users = []
        for p in range(4):
            pt = t1 * 4 + p
            kps = SBANK[pt % 3]
            vps = PBANK[pt % 2]
            kk = []
            for dk in range(16):
                def mk(dk=dk, p=p, kps=kps):
                    def f(e):
                        return e.matmul(kps[:, 0:512],
                                        XqT1[:, dk * 512 + p * 128:dk * 512 + p * 128 + 128],
                                        wkv_sb[:, dk * 1024:dk * 1024 + 512],
                                        start=(dk == 0), stop=(dk == 15))
                    return f
                d0 = (xq_ready + [e_wkv] + bfree(kps)) if dk == 0 else [kk[-1]]
                kk.append(sch.add("T", mk(), deps=d0))
            ekev = sch.add("A", lambda e, kps=kps: e.copy(out=ktmp[:], in_=kps[:, 0:512]),
                           deps=[kk[-1]] + prev_ktmp_readers)
            bset(kps, [ekev])
            lo_ap = bass.AP(ktmp, 0, [[512, 128], [128, 4], [1, 64]])
            hi_ap = bass.AP(ktmp, 64, [[512, 128], [128, 4], [1, 64]])
            def cs(t, pt):
                return bass.AP(t, pt * 64, [[NPT1 * 64, 128], [0, 4], [1, 64]])
            r1 = sch.add("V", lambda e, pt=pt, lo_ap=lo_ap: e.tensor_tensor(
                out=ropeT[:, 0:256], in0=lo_ap, in1=cs(cosk, pt), op=ALU.mult), deps=[ekev] + ev_tabk)
            r2 = sch.add("V", lambda e, pt=pt, hi_ap=hi_ap: e.tensor_tensor(
                out=ropeT[:, 256:512], in0=hi_ap, in1=cs(sink, pt), op=ALU.mult), deps=[ekev])
            r3 = sch.add("V", lambda e, pt=pt, lo_ap=lo_ap: e.tensor_tensor(
                out=ropeT[:, 512:768], in0=lo_ap, in1=cs(sink, pt), op=ALU.mult), deps=[ekev])
            r4 = sch.add("V", lambda e, pt=pt, hi_ap=hi_ap: e.tensor_tensor(
                out=ropeT[:, 768:1024], in0=hi_ap, in1=cs(cosk, pt), op=ALU.mult), deps=[ekev])
            r5 = sch.add("V", lambda e, lo_ap=lo_ap: e.tensor_tensor(
                out=lo_ap, in0=ropeT[:, 0:256], in1=ropeT[:, 256:512], op=ALU.subtract), deps=[r1, r2, r3])
            r6 = sch.add("V", lambda e, hi_ap=hi_ap: e.tensor_tensor(
                out=hi_ap, in0=ropeT[:, 512:768], in1=ropeT[:, 768:1024], op=ALU.add), deps=[r4, r5])
            bank = talloc()
            ktr = []
            for g in range(4):
                def mk(g=g, bank=bank):
                    def f(e):
                        return e.transpose(bank[:, g * 128:(g + 1) * 128],
                                           ktmp[:, g * 128:(g + 1) * 128], eye_sb[:])
                    return f
                ktr.append(sch.add("T", mk(), deps=[r5, r6] + bfree(bank)))
            def mkev(pt=pt, bank=bank):
                def f(e):
                    dst = bass.AP(K_sb, pt * 128, [[4 * S, 128], [S, 4], [1, 128]])
                    return e.tensor_copy(out=dst, in_=bank[:, 0:512])
                return f
            ekv = sch.add("V", mkev(), deps=ktr)
            bset(bank, [ekv])
            evK[pt] = ekv
            prev_ktmp_readers = ktr
            vv = []
            for dk in range(16):
                def mk(dk=dk, p=p, vps=vps):
                    def f(e):
                        return e.matmul(vps[:, 0:512],
                                        XqT1[:, dk * 512 + p * 128:dk * 512 + p * 128 + 128],
                                        wkv_sb[:, dk * 1024 + 512:dk * 1024 + 1024],
                                        start=(dk == 0), stop=(dk == 15))
                    return f
                d0 = (xq_ready + [e_wkv] + bfree(vps)) if dk == 0 else [vv[-1]]
                vv.append(sch.add("T", mk(), deps=d0))
            evev = sch.add("A", lambda e, vps=vps, pt=pt: e.copy(
                out=V_sb[:, pt * 512:(pt + 1) * 512], in_=vps[:, 0:512]), deps=[vv[-1]])
            bset(vps, [evev])
            evV[pt] = evev
            prev_xq_users += [kk[-1], vv[-1]]

    ph1_bar = [sch.last("T"), sch.last("V"), sch.last("A"), e_wkv]
    stack1.close()

    xn_sb = sb("xn_sb", [128, 4 * D], BF16)
    xnT_sb = sb("xnT_sb", [128, 16 * 128], BF16)
    XqT2 = sb("XqT2", [128, 16 * 512], BF16)
    Q_sb = sb("Q_sb", [128, 4 * D], BF16)
    attn_sb = sb("attn_sb", [128, 4 * D], BF16)
    PT_sb = sb("PT_sb", [128, 3 * 512], BF16)
    mask_sb = sb("mask_sb", [128, 8 * 512], BF16)
    wst_sb = sb("wst_sb", [128, 2 * 1024], BF16)
    x2_sb = sb("x2_sb", [128, 2 * D], F32)
    out_sb = x2_sb

    ev_stores = []
    prev_unit = list(ph1_bar)
    prev_strip_reader = {0: None, 1: None}
    strip_ct = {"i": 0}
    slot_users = {0: [], 1: [], 2: []}
    prev_out_store = {}
    prev_outT_readers = []

    for u in range(UNITS):
        TT = RUN[u] // 128
        mstart = TT - 8
        e_mload = sch.add("DS", lambda e, u=u: e.dma_start(
            out=bass.AP(mask_sb, 0, [[8 * 512, 128], [512, 8], [1, 512]]),
            in_=bass.AP(masks_h, u * 8 * 128 * 512, [[512, 128], [128 * 512, 8], [1, 512]])),
            deps=prev_unit)

        ev_XqT = []
        ev_ffn_done = []
        prev_x2_reader = {0: None, 1: None}
        for p in range(4):
            ptg = u * 4 + p
            buf = p % 2
            x_ap = x2_sb[:, buf * D:(buf + 1) * D]
            eld = sch.add("DS", lambda e, ptg=ptg, x_ap=x_ap: e.dma_start(
                out=x_ap, in_=xq_own.ap()[ptg * 128:(ptg + 1) * 128, :]),
                deps=[prev_x2_reader[buf]] + prev_unit)
            eq, exnb = ln_quant(eld, x_ap, xn_sb[:, p * D:(p + 1) * D], prev_unit)
            es1 = sch.add("V", lambda e: e.tensor_scalar(out=svec[:, 5:6], in0=svec[:, 1:2],
                                                         scalar1=svec[:, 2:3], scalar2=svec[:, 2:3],
                                                         op0=ALU.mult, op1=ALU.mult), deps=[eq], sdeps=[eq])
            es2 = sch.add("V", lambda e: e.tensor_scalar(out=svec[:, 5:6], in0=svec[:, 5:6],
                                                         scalar1=2048.0, scalar2=None, op0=ALU.mult),
                          deps=[es1])
            ev_XqT += transpose_tile(scr_c, 0, eq, XqT2, lambda k, p=p: k * 512 + p * 128, extra=prev_unit)
            ev_xnT_p = transpose_tile(xn_sb, p * D, exnb, xnT_sb, lambda k: k * 128,
                                      extra=prev_unit + ev_ffn_done[-1:])
            fsel = []
            for dk in range(16):
                def mk(dk=dk, p=p):
                    def f(e):
                        return e.matmul(M0[:, 0:136],
                                        xnT_sb[:, dk * 128:(dk + 1) * 128],
                                        ffnW[:, dk * 136:(dk + 1) * 136],
                                        start=(dk == 0), stop=(dk == 15))
                    return f
                d0 = (ev_xnT_p + bfree(M0) + e_fw) if dk == 0 else [fsel[-1]]
                fsel.append(sch.add("T", mk(), deps=d0))
            g0 = sch.add("V", lambda e: e.tensor_tensor(out=gate_s[:, 0:64], in0=M0[:, 0:64],
                                                        in1=cselb[:], op=ALU.subtract),
                         deps=[fsel[-1], e_csel])
            g1 = sch.add("V", lambda e: e.tensor_tensor(out=gate_s[:, 0:64], in0=gate_s[:, 0:64],
                                                        in1=gate_s[:, 0:64], op=ALU.mult), deps=[g0])
            g2 = sch.add("V", lambda e: e.tensor_reduce(
                out=gate_s[:, 64:72], in_=bass.AP(gate_s, 0, [[128, 128], [8, 8], [1, 8]]),
                axis=AX.X, op=ALU.add), deps=[g1])
            g3 = sch.add("V", lambda e: e.scalar_tensor_tensor(out=gate_s[:, 72:80], in0=M0[:, 128:136],
                                                               scalar=-2.0, in1=musqb[:], op0=ALU.mult,
                                                               op1=ALU.add), deps=[fsel[-1], e_musq])
            g4 = sch.add("V", lambda e: e.tensor_scalar(out=gate_s[:, 72:80], in0=gate_s[:, 72:80],
                                                        scalar1=svec[:, 5:6], scalar2=None, op0=ALU.add),
                         deps=[g3], sdeps=[es2])
            g5 = sch.add("V", lambda e: e.reciprocal(out=gate_s[:, 72:80], in_=gate_s[:, 72:80]), deps=[g4])
            g6 = sch.add("V", lambda e: e.tensor_tensor(out=gate_s[:, 64:72], in0=gate_s[:, 64:72],
                                                        in1=gate_s[:, 72:80], op=ALU.mult), deps=[g5, g2])
            g7 = sch.add("V", lambda e: e.tensor_tensor(out=gate_s[:, 64:72], in0=gate_s[:, 64:72],
                                                        in1=centb_sb[:], op=ALU.subtract), deps=[g6, e_cent])
            g8 = sch.add("V", lambda e: e.tensor_tensor(out=gate_s[:, 64:72], in0=gate_s[:, 64:72],
                                                        in1=slopeb_sb[:], op=ALU.mult), deps=[g7, e_slope])
            g9 = sch.add("A", lambda e: e.activation(out=gate_s[:, 80:88], in_=gate_s[:, 64:72],
                                                     func=AF.Sigmoid), deps=[g8])
            g10 = sch.add("V", lambda e: e.reduce_max(out=svec[:, 6:7], in_=gate_s[:, 80:88], axis=AX.X),
                          deps=[g9])
            g11 = sch.add("V", lambda e: e.tensor_tensor(
                out=gate_s[:, 88:96], in0=gate_s[:, 80:88],
                in1=bass.AP(svec, 6, [[16, 128], [0, 8]]), op=ALU.is_equal), deps=[g10])
            g12 = sch.add("V", lambda e: e.tensor_tensor(out=gate_s[:, 88:96], in0=gate_s[:, 88:96],
                                                         in1=nmBb_sb[:], op=ALU.mult), deps=[g11, e_nmB])
            g13 = sch.add("V", lambda e: e.tensor_scalar(out=gate_s[:, 88:96], in0=gate_s[:, 88:96],
                                                         scalar1=1000.0, scalar2=None, op0=ALU.add), deps=[g12])
            g14 = sch.add("V", lambda e: e.tensor_reduce(out=svec[:, 7:8], in_=gate_s[:, 88:96],
                                                         axis=AX.X, op=ALU.min), deps=[g13])
            g15 = sch.add("V", lambda e: e.tensor_tensor(
                out=scr_c[:, 64:72], in0=nvalsb_sb[:],
                in1=bass.AP(svec, 7, [[16, 128], [0, 8]]), op=ALU.is_equal), deps=[g14, e_nvals])
            g16 = sch.add("V", lambda e: e.tensor_tensor(out=gate_s[:, 0:64], in0=M0[:, 64:128],
                                                         in1=cupdb[:], op=ALU.subtract),
                          deps=[fsel[-1], e_cupd, g2])
            bset(M0, [g3, g16])
            g17 = sch.add("V", lambda e: e.tensor_tensor(
                out=scr_c[:, 0:64], in0=gate_s[:, 0:64],
                in1=bass.AP(scr_c, 64, [[D, 128], [0, 8], [1, 8]]), op=ALU.mult), deps=[g16, g15])
            bank = talloc()
            g18 = sch.add("T", lambda e, bank=bank: e.transpose(bank[0:72, 0:128], scr_c[:, 0:72],
                                                                eye_sb[:]), deps=[g17] + bfree(bank))
            g19 = sch.add("V", lambda e, bank=bank: e.tensor_copy(out=zaugT_sb[0:72, 0:128],
                                                                  in_=bank[0:72, 0:128]), deps=[g18])
            bset(bank, [g19])
            modps = SBANK[p % 3]
            last_f = None
            for ncix in range(4):
                def mk(ncix=ncix, modps=modps):
                    def f(e):
                        return e.matmul(modps[:, 0:512], zaugT_sb[0:72, 0:128],
                                        w2aug_sb[0:72, ncix * 512:(ncix + 1) * 512],
                                        start=True, stop=True)
                    return f
                dmm = [g19] + e_fw + (bfree(modps) if ncix == 0 else [last_f])
                emm = sch.add("T", mk(), deps=dmm)
                def mkf1(ncix=ncix, p=p, modps=modps):
                    def f(e):
                        return e.tensor_tensor(out=scr_a[:, ncix * 512:(ncix + 1) * 512],
                                               in0=modps[:, 0:512],
                                               in1=xn_sb[:, p * D + ncix * 512:p * D + (ncix + 1) * 512],
                                               op=ALU.subtract)
                    return f
                ef1 = sch.add("V", mkf1(), deps=[emm, exnb, eq])
                def mkf2(ncix=ncix, p=p):
                    def f(e):
                        sl0 = p * D + ncix * 512
                        return e.scalar_tensor_tensor(out=xn_sb[:, sl0:sl0 + 512],
                                                      in0=scr_a[:, ncix * 512:(ncix + 1) * 512],
                                                      scalar=svec[:, 6:7],
                                                      in1=xn_sb[:, sl0:sl0 + 512],
                                                      op0=ALU.mult, op1=ALU.add)
                    return f
                ef2 = sch.add("V", mkf2(), deps=[ef1, ev_xnT_p[-1]], sdeps=[g10])
                def mkf3(ncix=ncix, p=p, buf=buf):
                    def f(e):
                        sl0 = p * D + ncix * 512
                        return e.tensor_tensor(out=xn_sb[:, sl0:sl0 + 512], in0=xn_sb[:, sl0:sl0 + 512],
                                               in1=x2_sb[:, buf * D + ncix * 512:buf * D + (ncix + 1) * 512],
                                               op=ALU.add)
                    return f
                ef3 = sch.add("V", mkf3(), deps=[ef2])
                last_f = ef3
            bset(modps, [last_f])
            ev_ffn_done.append(last_f)
            prev_x2_reader[buf] = last_f

        ev_rope_q = {}
        qps_l = [S0, S1, S2, P0]
        mm_last = {}
        for dnc in range(4):
            for dsub in range(8):
                slot = strip_ct["i"] % 2
                strip_ct["i"] += 1
                def mkld(dnc=dnc, dsub=dsub, slot=slot):
                    def f(e):
                        src = bass.AP(wq_bf, dsub * 256 * D + dnc * 512,
                                      [[D, 128], [128 * D, 2], [1, 512]])
                        dst = bass.AP(wst_sb, slot * 1024, [[2 * 1024, 128], [512, 2], [1, 512]])
                        return e.dma_start(out=dst, in_=src)
                    return f
                eldw = sch.add("DS", mkld(), deps=[e_wqbf, prev_strip_reader[slot]] + prev_unit)
                for dk in range(2):
                    for p in range(4):
                        kglob = dsub * 2 + dk
                        def mk(dk=dk, p=p, kglob=kglob, slot=slot):
                            def f(e):
                                return e.matmul(qps_l[p][:, 0:512],
                                                XqT2[:, kglob * 512 + p * 128:kglob * 512 + p * 128 + 128],
                                                wst_sb[:, slot * 1024 + dk * 512:slot * 1024 + (dk + 1) * 512],
                                                start=(kglob == 0), stop=(kglob == 15))
                            return f
                        dps = [eldw]
                        if kglob == 0:
                            dps += ev_XqT + bfree(qps_l[p])
                        else:
                            dps += [mm_last[p]]
                        mm_last[p] = sch.add("T", mk(), deps=dps)
                        if dk == 1 and p == 3:
                            prev_strip_reader[slot] = mm_last[3]
            for p in range(4):
                def mkev(p=p, dnc=dnc):
                    def f(e):
                        return e.copy(out=Q_sb[:, p * D + dnc * 512:p * D + (dnc + 1) * 512],
                                      in_=qps_l[p][:, 0:512])
                    return f
                eev = sch.add("A", mkev(), deps=[mm_last[p]])
                bset(qps_l[p], [eev])
                lo = bass.AP(Q_sb, p * D + dnc * 512, [[4 * D, 128], [128, 4], [1, 64]])
                hi = bass.AP(Q_sb, p * D + dnc * 512 + 64, [[4 * D, 128], [128, 4], [1, 64]])
                def cq(t, u=u, p=p):
                    return bass.AP(t, (u * 4 + p) * 64, [[16 * 64, 128], [0, 4], [1, 64]])
                q1 = sch.add("V", lambda e, lo=lo, p=p, u=u: e.tensor_tensor(
                    out=ropeT[:, 0:256], in0=lo, in1=cq(cosq, u, p), op=ALU.mult), deps=[eev] + ev_tabq)
                q2 = sch.add("V", lambda e, hi=hi, p=p, u=u: e.tensor_tensor(
                    out=ropeT[:, 256:512], in0=hi, in1=cq(sinq, u, p), op=ALU.mult), deps=[eev])
                q3 = sch.add("V", lambda e, lo=lo, p=p, u=u: e.tensor_tensor(
                    out=ropeT[:, 512:768], in0=lo, in1=cq(sinq, u, p), op=ALU.mult), deps=[eev])
                q4 = sch.add("V", lambda e, hi=hi, p=p, u=u: e.tensor_tensor(
                    out=ropeT[:, 768:1024], in0=hi, in1=cq(cosq, u, p), op=ALU.mult), deps=[eev])
                q5 = sch.add("V", lambda e, lo=lo: e.tensor_tensor(
                    out=lo, in0=ropeT[:, 0:256], in1=ropeT[:, 256:512], op=ALU.subtract), deps=[q1, q2, q3])
                q6 = sch.add("V", lambda e, hi=hi: e.tensor_tensor(
                    out=hi, in0=ropeT[:, 512:768], in1=ropeT[:, 768:1024], op=ALU.add), deps=[q4, q5])
                ev_rope_q[(p, dnc)] = q6

        last_q_mms = [prev_strip_reader[0], prev_strip_reader[1]]
        ev_QT = []
        for hpair in range(8):
            bank = talloc()
            trs = []
            for hh in range(2):
                h = hpair * 2 + hh
                dnc = h // 4
                for p in range(4):
                    def mk(h=h, p=p, hh=hh, bank=bank):
                        def f(e):
                            return e.transpose(bank[:, hh * 512 + p * 128:hh * 512 + (p + 1) * 128],
                                               Q_sb[:, p * D + h * 128:p * D + (h + 1) * 128], eye_sb[:])
                        return f
                    trs.append(sch.add("T", mk(), deps=[ev_rope_q[(p, dnc)]] + bfree(bank) + last_q_mms))
            def mkev(hpair=hpair, bank=bank):
                def f(e):
                    return e.tensor_copy(out=XqT2[:, hpair * 1024:(hpair + 1) * 1024], in_=bank[:, 0:1024])
                return f
            ev = sch.add("V", mkev(), deps=trs)
            bset(bank, [ev])
            ev_QT.append(ev)

        ev_attn_evac = {}
        ev_rd = {}
        for h in range(16):
            g = h // 4
            pvb = PBANK[h % 2]
            denom_prev = None
            pv_prev = None
            for tt in range(TT):
                sbx = tt % 3
                sbk = SBANK[sbx]
                def mkS(h=h, tt=tt, g=g, sbk=sbk):
                    def f(e):
                        return e.matmul(sbk[:, 0:512], K_sb[:, g * S + tt * 128:g * S + (tt + 1) * 128],
                                        XqT2[:, h * 512:(h + 1) * 512], start=True, stop=True)
                    return f
                eS = sch.add("T", mkS(), deps=[evK[tt]] + ev_QT + bfree(sbk))
                def mkE(sbx=sbx, sbk=sbk):
                    def f(e):
                        return e.activation(out=PT_sb[:, sbx * 512:(sbx + 1) * 512], in_=sbk[:, 0:512],
                                            func=AF.Exp, scale=consts[:, 0:1])
                    return f
                eE = sch.add("A", mkE(), deps=[eS, ec2] + slot_users[sbx])
                bset(sbk, [eE])
                slot_users[sbx] = []
                last_pt = eE
                if tt >= mstart:
                    m = tt - mstart
                    def mkM(sbx=sbx, m=m):
                        def f(e):
                            sl = slice(sbx * 512, (sbx + 1) * 512)
                            return e.tensor_tensor(out=PT_sb[:, sl], in0=PT_sb[:, sl],
                                                   in1=mask_sb[:, m * 512:(m + 1) * 512], op=ALU.mult)
                        return f
                    last_pt = sch.add("V", mkM(), deps=[eE, e_mload])
                def mkD(sbx=sbx, tt=tt, TT=TT):
                    def f(e):
                        return e.matmul(M0[0:1, 0:512], ones_sb[:, 0:1],
                                        PT_sb[:, sbx * 512:(sbx + 1) * 512],
                                        start=(tt == 0), stop=(tt == TT - 1))
                    return f
                dD = [last_pt, e_ones] + (bfree(M0) if tt == 0 else [denom_prev])
                eD = sch.add("T", mkD(), deps=dD)
                denom_prev = eD
                def mkP(tt=tt, g=g, sbx=sbx, pvb=pvb, TT=TT):
                    def f(e):
                        return e.matmul(pvb[:, 0:512], V_sb[:, tt * 512 + g * 128:tt * 512 + g * 128 + 128],
                                        PT_sb[:, sbx * 512:(sbx + 1) * 512],
                                        start=(tt == 0), stop=(tt == TT - 1))
                    return f
                dP = [last_pt, evV[tt], eD] + (bfree(pvb) if tt == 0 else [pv_prev])
                eP = sch.add("T", mkP(), deps=dP)
                pv_prev = eP
                slot_users[sbx] = [eD, eP]
            eOv = sch.add("A", lambda e, pvb=pvb: e.copy(out=outT_sb[:], in_=pvb[:, 0:512]),
                          deps=[pv_prev] + prev_outT_readers)
            eDe = sch.add("V", lambda e: e.tensor_copy(out=scr_a[0:1, 0:512], in_=M0[0:1, 0:512]),
                          deps=[denom_prev])
            bset(M0, [eDe])
            dtr = []
            for qc in range(4):
                def mkT(qc=qc, pvb=pvb):
                    def f(e):
                        return e.transpose(pvb[0:128, qc:qc + 1], scr_a[0:1, qc * 128:(qc + 1) * 128],
                                           eye32_sb[0:1, 0:1])
                    return f
                dtr.append(sch.add("T", mkT(), deps=[eDe, e_eye32, eOv]))
            erc = sch.add("V", lambda e, h=h, pvb=pvb: e.reciprocal(out=rd_all[:, h * 4:h * 4 + 4],
                                                                    in_=pvb[0:128, 0:4]), deps=dtr)
            bset(pvb, [erc])
            ev_rd[h] = erc
            bank = talloc()
            otr = []
            for qc in range(4):
                def mkO(qc=qc, bank=bank):
                    def f(e):
                        return e.transpose(bank[:, qc * 128:(qc + 1) * 128],
                                           outT_sb[:, qc * 128:(qc + 1) * 128], eye_sb[:])
                    return f
                otr.append(sch.add("T", mkO(), deps=[eOv] + bfree(bank)))
            prev_outT_readers = otr
            def mkAv(h=h, bank=bank):
                def f(e):
                    dst = bass.AP(attn_sb, h * 128, [[4 * D, 128], [D, 4], [1, 128]])
                    return e.tensor_copy(out=dst, in_=bank[:, 0:512])
                return f
            eAv = sch.add("V", mkAv(), deps=otr)
            bset(bank, [eAv])
            ev_attn_evac[h] = eAv

        ev_post = {qc: [] for qc in range(4)}
        for h in range(16):
            for qc in range(4):
                def mkp(h=h, qc=qc):
                    def f(e):
                        sl0 = qc * D + h * 128
                        return e.tensor_scalar(out=attn_sb[:, sl0:sl0 + 128], in0=attn_sb[:, sl0:sl0 + 128],
                                               scalar1=rd_all[:, h * 4 + qc:h * 4 + qc + 1],
                                               scalar2=None, op0=ALU.mult)
                    return f
                ev_post[qc].append(sch.add("V", mkp(), deps=[ev_attn_evac[h]], sdeps=[ev_rd[h]]))

        ev_XoT = []
        ev_last_dtr = dtr
        for qc in range(4):
            esq = sch.add("A", lambda e, qc=qc: e.activation(out=scr_c[:], in_=attn_sb[:, qc * D:(qc + 1) * D],
                                                             func=AF.Square, accum_out=svec[:, 0:1]),
                          deps=ev_post[qc] + (ev_XoT[-2:] if ev_XoT else []))
            er1 = sch.add("V", lambda e: e.tensor_scalar(out=svec[:, 1:2], in0=svec[:, 0:1],
                                                         scalar1=INV2048, scalar2=consts[:, 4:5],
                                                         op0=ALU.mult, op1=ALU.add), deps=[esq, ec0])
            er2 = sch.add("A", lambda e: e.activation(out=svec[:, 1:2], in_=svec[:, 1:2], func=AF.Sqrt),
                          deps=[er1])
            er3 = sch.add("V", lambda e: e.reciprocal(out=svec[:, 1:2], in_=svec[:, 1:2]), deps=[er2])
            ean = sch.add("V", lambda e, qc=qc: e.scalar_tensor_tensor(
                out=scr_a[:], in0=attn_sb[:, qc * D:(qc + 1) * D], scalar=svec[:, 1:2],
                in1=normw_sb[:], op0=ALU.mult, op1=ALU.mult),
                deps=[esq, e_normw] + (ev_last_dtr if qc == 0 else []), sdeps=[er3])
            eq2 = quant_only(ean)
            ev_XoT += transpose_tile(scr_c, 0, eq2, Q_sb, lambda k, qc=qc: k * 512 + qc * 128)

        wps_l = [S0, S1, S2, P0]
        wo_last = {}
        for dnc in range(4):
            for dsub in range(8):
                slot = strip_ct["i"] % 2
                strip_ct["i"] += 1
                def mkld(dnc=dnc, dsub=dsub, slot=slot):
                    def f(e):
                        src = bass.AP(wo_bf, dsub * 256 * D + dnc * 512,
                                      [[D, 128], [128 * D, 2], [1, 512]])
                        dst = bass.AP(wst_sb, slot * 1024, [[2 * 1024, 128], [512, 2], [1, 512]])
                        return e.dma_start(out=dst, in_=src)
                    return f
                eldw = sch.add("DS", mkld(), deps=[e_wobf, prev_strip_reader[slot]])
                for dk in range(2):
                    for qc in range(4):
                        kglob = dsub * 2 + dk
                        def mk(dk=dk, qc=qc, kglob=kglob, slot=slot):
                            def f(e):
                                return e.matmul(wps_l[qc][:, 0:512],
                                                Q_sb[:, kglob * 512 + qc * 128:kglob * 512 + qc * 128 + 128],
                                                wst_sb[:, slot * 1024 + dk * 512:slot * 1024 + (dk + 1) * 512],
                                                start=(kglob == 0), stop=(kglob == 15))
                            return f
                        dps = [eldw]
                        if kglob == 0:
                            dps += ev_XoT + bfree(wps_l[qc])
                        else:
                            dps += [wo_last[qc]]
                        wo_last[qc] = sch.add("T", mk(), deps=dps)
                        if dk == 1 and qc == 3:
                            prev_strip_reader[slot] = wo_last[3]
            for qc in range(4):
                slot8 = (qc * 4 + dnc) % 8
                def mkev(qc=qc, dnc=dnc, slot8=slot8):
                    def f(e):
                        return e.tensor_tensor(out=out_sb[:, slot8 * 512:(slot8 + 1) * 512],
                                               in0=wps_l[qc][:, 0:512],
                                               in1=xn_sb[:, qc * D + dnc * 512:qc * D + (dnc + 1) * 512],
                                               op=ALU.add)
                    return f
                dstore = [prev_out_store.get(slot8)]
                eev = sch.add("V", mkev(), deps=[wo_last[qc], ev_ffn_done[qc]] + dstore)
                bset(wps_l[qc], [eev])
                def mkst(qc=qc, u=u, dnc=dnc, slot8=slot8):
                    def f(e):
                        row0 = u * 512 + qc * 128
                        dst = bass.AP(out_h, row0 * D + dnc * 512, [[D, 128], [1, 512]])
                        return e.dma_start(out=dst, in_=out_sb[:, slot8 * 512:(slot8 + 1) * 512])
                    return f
                est = sch.add("DG", mkst(), deps=[eev])
                prev_out_store[slot8] = est
                ev_stores.append(est)
        prev_unit = [sch.last("T"), sch.last("V"), sch.last("A"), ev_stores[-1]]

    sch.add("DS", lambda e: e.nop(), deps=ev_stores)

    sch.finalize()
    sems = {}
    semstack = contextlib.ExitStack()
    for c in ("T", "V", "A"):
        sems[c] = semstack.enter_context(nc.semaphore("sem" + c))
    for c in ("DS", "DG"):
        sems[c] = [semstack.enter_context(nc.semaphore("sem" + c + str(s))) for s in range(Sch.NSLOT)]
    with nc.Block() as block:
        @block.tensor
        def _(eng):
            sch.emit("T", eng, sems)

        @block.vector
        def _(eng):
            sch.emit("V", eng, sems)

        @block.scalar
        def _(eng):
            sch.emit("A", eng, sems)

        @block.sync
        def _(eng):
            sch.emit("DS", eng, sems)

        @block.gpsimd
        def _(eng):
            sch.emit("DG", eng, sems)
    semstack.close()
    stack.close()
    return nc


_NC = None


def _get_nc():
    global _NC
    if _NC is None:
        _NC = build_nc()
    return _NC


def kernel(L0_attn_x, L0_attn_cache_k, L0_attn_cache_v, L0_attn_wqkv, L0_attn_wo, L0_attn_norm_w,
           L0_ffn_select_weight, L0_ffn_select_mean_diff, L0_ffn_imp_center, L0_ffn_imp_slope,
           L0_ffn_lora_update, L0_ffn_lora_degen, L0_ffn_bias_w, L0_ffn_debias_w):
    x = np.asarray(L0_attn_x, dtype=np.float32)
    wqkv = np.asarray(L0_attn_wqkv, dtype=np.float32)
    wo = np.asarray(L0_attn_wo, dtype=np.float32)
    norm_w = np.asarray(L0_attn_norm_w, dtype=np.float32)
    selw = np.asarray(L0_ffn_select_weight, dtype=np.float32)
    mud = np.asarray(L0_ffn_select_mean_diff, dtype=np.float32)
    impc = np.asarray(L0_ffn_imp_center, dtype=np.float32)
    imps = np.asarray(L0_ffn_imp_slope, dtype=np.float32)
    lup = np.asarray(L0_ffn_lora_update, dtype=np.float32)
    ldg = np.asarray(L0_ffn_lora_degen, dtype=np.float32)
    biw = np.asarray(L0_ffn_bias_w, dtype=np.float32)
    dbw = np.asarray(L0_ffn_debias_w, dtype=np.float32)

    B = x.shape[0]

    perm128 = np.concatenate([np.arange(0, 128, 2), np.arange(1, 128, 2)])
    wqkvT = np.ascontiguousarray(wqkv.T)
    cols = np.arange(3072)
    for blk in range(20):
        base = blk * 128
        if base >= 2560:
            break
        cols[base:base + 128] = base + perm128
    wqkvT = wqkvT[:, cols]
    wqT_a = np.ascontiguousarray(wqkvT[:, 0:2048])
    wkvT_a = np.ascontiguousarray(wqkvT[:, 2048:3072])
    woT_a = np.ascontiguousarray(wo.T)

    normw_a = np.ascontiguousarray(np.broadcast_to(norm_w[None, :], (128, D)))
    eye_a = np.eye(128, dtype=ml_dtypes.bfloat16)
    eye32_a = np.eye(128, dtype=np.float32)
    tiles = np.arange(NPT1)
    pp = np.arange(128)
    posk_a = np.ascontiguousarray(
        np.broadcast_to((tiles[None, :, None] * 128 + pp[:, None, None]).astype(np.float32),
                        (128, NPT1, 64)).reshape(128, NPT1 * 64))
    jrow_a = np.ascontiguousarray(np.broadcast_to(np.arange(64, dtype=np.float32)[None, :], (128, 64)))
    wselT_a = np.ascontiguousarray(selw.transpose(1, 0, 2).reshape(D, 64))
    updT_a = np.ascontiguousarray(lup.transpose(1, 2, 0).reshape(D, 64))
    w2aug_a = np.concatenate([ldg.transpose(2, 0, 1).reshape(64, D), biw], axis=0)
    w2aug_a = np.ascontiguousarray(w2aug_a)
    mudT_a = np.ascontiguousarray(mud.T)
    debT_a = np.ascontiguousarray(dbw.T)
    slopeb_a = np.ascontiguousarray(np.broadcast_to(imps[None, :], (128, 8)))
    centb_a = np.ascontiguousarray(np.broadcast_to(impc[None, :], (128, 8)))
    nvals_a = np.ascontiguousarray(np.broadcast_to(np.arange(8, dtype=np.float32)[None, :], (128, 8)))
    nmB_a = np.ascontiguousarray(nvals_a - 1000.0)
    nn = np.arange(8)
    diagns_a = np.ascontiguousarray(
        (np.repeat(nn, 8)[:, None] == nn[None, :]).astype(ml_dtypes.bfloat16))
    diaghn_a = np.ascontiguousarray(
        (np.tile(nn, 8)[:, None] == nn[None, :]).astype(ml_dtypes.bfloat16))

    nc = _get_nc()

    in_maps = []
    for c in range(NCORE := 8):
        b = c // 2
        par = c % 2
        chunks = CHUNKS[par]
        rows = np.concatenate([np.arange(ch * 512, (ch + 1) * 512) for ch in chunks])
        xq_own_a = np.ascontiguousarray(x[b][rows])
        posq_a = np.ascontiguousarray(
            np.broadcast_to(
                (rows.reshape(16, 128).T[:, :, None]).astype(np.float32), (128, 16, 64)
            ).reshape(128, 16 * 64))
        mk = np.zeros((UNITS, 8, 128, 512), dtype=ml_dtypes.bfloat16)
        for j in range(UNITS):
            q0 = chunks[j] * 512
            for m in range(8):
                t0 = RUN[j] - 1024 + m * 128
                tloc = t0 + np.arange(128)
                qloc = q0 + np.arange(512)
                mk[j, m] = (tloc[:, None] <= qloc[None, :]).astype(ml_dtypes.bfloat16)
        im = {
            "xin": np.ascontiguousarray(x[b]),
            "xq_own": xq_own_a,
            "wqT": wqT_a, "wkvT": wkvT_a, "woT": woT_a,
            "normw": normw_a, "eye": eye_a, "eye32": eye32_a,
            "posk": posk_a, "posq": posq_a, "jrow": jrow_a,
            "masks": np.ascontiguousarray(mk.reshape(UNITS * 8 * 128, 512)),
            "wselT": wselT_a, "updT": updT_a, "w2aug": w2aug_a,
            "mudT": mudT_a, "debT": debT_a,
            "slopeb": slopeb_a, "centb": centb_a, "nvalsb": nvals_a, "nmBb": nmB_a,
            "diagns": diagns_a, "diaghn": diaghn_a,
        }
        in_maps.append(im)

    res = run_bass_kernel_spmd(nc, in_maps, core_ids=list(range(8)))
    out = np.empty((B, S, D), dtype=np.float32)
    for c in range(8):
        b = c // 2
        par = c % 2
        chunks = CHUNKS[par]
        o = res.results[c]["out"]
        for j, ch in enumerate(chunks):
            out[b, ch * 512:(ch + 1) * 512] = o[j * 512:(j + 1) * 512]
    return out
